# version 20
# baseline (speedup 1.0000x reference)
import numpy as np
from contextlib import ExitStack

import concourse.bass as bass
import concourse.mybir as mybir
from concourse.bass_utils import run_bass_kernel_spmd

F32 = mybir.dt.float32
BF16 = mybir.dt.bfloat16
AF = mybir.ActivationFunctionType
ALU = mybir.AluOpType
AX = mybir.AxisListType.X
AP = bass.AP

D, DC = 1024, 8
HID, HC = 4096, 32
NH, DH = 16, 64
DL, LC = 256, 2
LN_EPS = 1e-5

ENGS = ("sync", "gpsimd", "tensor", "vector", "scalar")


class Rot:

    def __init__(self, b, banks, sem, bases=None):
        self.b, self.banks, self.sem = b, banks, sem
        self.pending = {}
        self.i = 0
        if bases is None:
            bases = (b.count[b.s_ev], b.count[b.s_v])
        b.wait("tensor", b.s_ev, bases[0])
        b.wait("tensor", b.s_v, bases[1])

    def acquire(self):
        bank = self.banks[self.i % len(self.banks)]
        if bank in self.pending:
            self.b.wait("tensor", self.sem, self.pending[bank])
        self.i += 1
        return bank

    def release(self, bank):
        self.pending[bank] = self.b.count[self.sem]


class SlabStream:

    def __init__(self, b, eng, sems, nsl, mk, gate=None):
        self.b, self.eng, self.sems, self.nsl, self.mk = b, eng, sems, nsl, mk
        self.ldt = {}
        gsem, gval = gate if gate else (b.s_mm, b.count[b.s_mm])
        if gval:
            b.wait(eng, gsem, gval)
        for s in range(min(2, nsl)):
            dst, src = mk(s)
            self.ldt[s] = b.dma(eng, dst, src, sems[s % 2])

    def wait_loaded(self, s, eng="tensor"):
        self.b.wait(eng, self.sems[s % 2], self.ldt[s])

    def done(self, s, gate_sem, gate_val=None):
        if s + 2 < self.nsl:
            if gate_val is None:
                gate_val = self.b.count[gate_sem]
            self.b.wait(self.eng, gate_sem, gate_val)
            dst, src = self.mk(s + 2)
            self.ldt[s + 2] = self.b.dma(self.eng, dst, src, self.sems[s % 2])


class Builder:
    def __init__(self, T, phases, taps):
        self.T = T
        self.phases = phases
        self.taps = taps
        self.nc = bass.Bass("TRN2", target_bir_lowering=False)
        self.ops = {e: [] for e in ENGS}
        self.semnames = []
        self.count = {}
        self.waited = {}
        self.nbar = 0

    def sem(self, name):
        assert name not in self.count
        self.semnames.append(name)
        self.count[name] = 0
        return name

    def raw(self, eng, fn):
        self.ops[eng].append(fn)

    def op(self, eng, meth, *args, inc=None, drain=True, **kwargs):
        tgt = None
        if inc is not None:
            self.count[inc] += 1
            tgt = self.count[inc]

        def fn(e, S, meth=meth, args=args, kwargs=kwargs, inc=inc, drain=drain):
            ins = getattr(e, meth)(*args, **kwargs)
            if inc is not None:
                ins.then_inc(S[inc], 1)
            if drain:
                e.drain()
        self.raw(eng, fn)
        return tgt

    def mm(self, out, lhsT, rhs, start, stop, inc=None):
        return self.op("tensor", "matmul", out, lhsT, rhs, start=start,
                       stop=stop, inc=inc, drain=False)

    def dma(self, eng, dst, src, sem):
        self.count[sem] += 16
        tgt = self.count[sem]

        def fn(e, S, dst=dst, src=src, sem=sem):
            e.dma_start(dst, src).then_inc(S[sem], 16)
        self.raw(eng, fn)
        return tgt

    def wait(self, eng, sem, val=None):
        if val is None:
            val = self.count[sem]
        seen = self.waited.setdefault(eng, {})
        if seen.get(sem, -1) >= val:
            return val
        seen[sem] = val

        def fn(e, S, sem=sem, val=val):
            e.wait_ge(S[sem], val)
        self.raw(eng, fn)
        return val

    def barrier(self, engs=ENGS, tag=""):
        self.nbar += 1
        name = self.sem(f"bar{self.nbar}{tag}")
        for e in engs:
            self.op(e, "drain", inc=name, drain=False)
        for e in engs:
            self.wait(e, name, len(engs))

    def finalize(self):
        nc = self.nc
        with ExitStack() as es:
            S = {n: es.enter_context(nc.semaphore(n)) for n in self.semnames}
            with nc.Block() as blk:
                for eng in ENGS:
                    def driver(e, _ops=self.ops[eng], _S=S):
                        for f in _ops:
                            f(e, _S)
                    getattr(blk, eng)(driver)
        return nc

    def declare(self):
        nc, T = self.nc, self.T
        TC = T // 128

        def din(name, shape):
            setattr(self, name, nc.dram_tensor(name, shape, F32, kind="ExternalInput"))

        din("x_t", [D, T])
        din("l0_wqt", [D, D]); din("l0_wkt", [D, D]); din("l0_wvt", [D, D])
        din("l0_bq", [128, DC]); din("l0_bk", [128, DC]); din("l0_bv_bc", [128, D])
        din("t1_raw", [D, HID]); din("t2_raw", [HID, D])
        din("l0_g1", [128, DC]); din("l0_b1v", [128, DC])
        din("l0_g2", [128, HC]); din("l0_b2v", [128, HC])
        din("l0_fb1", [128, HC]); din("l0_fb2", [128, DC])
        din("l1_wq", [D, D]); din("l1_bq_bc", [128, D])
        din("l1_wdown", [D, DL]); din("l1_bdown", [128, LC])
        din("l1_wup", [DL, 2 * D]); din("l1_bup_bc", [128, 2 * D])
        din("l1_wout", [D, D]); din("l1_bout", [128, DC])
        din("l1_w1", [D, HID]); din("l1_fb1", [128, HC])
        din("l1_w2", [HID, D]); din("l1_fb2", [128, DC])
        din("rope_snh", [128, TC * 64]); din("rope_csh", [128, TC * 64])

        self.y_t = nc.dram_tensor("y_t", [D, T], F32, kind="ExternalOutput")
        self.t1c = nc.dram_tensor("t1c", [D, HID], BF16, kind="Internal")
        self.t2c = nc.dram_tensor("t2c", [HID, D], BF16, kind="Internal")
        for nm, shp in [("l0_wqc", [D, D]), ("l0_wkc", [D, D]), ("l0_wvc", [D, D]),
                        ("l1_w1c", [D, HID]), ("l1_w2c", [HID, D]),
                        ("l1_wqc", [D, D]), ("l1_wdc", [D, DL]),
                        ("l1_wuc", [DL, 2 * D]), ("l1_woc", [D, D])]:
            setattr(self, nm, nc.dram_tensor(nm, shp, BF16, kind="Internal"))
        self.tap_t = {i: nc.dram_tensor(f"tap{i}", [D, T], F32, kind="ExternalOutput")
                      for i in self.taps}

        alloc = nc.alloc_sbuf_tensor
        self.x_sb = alloc("x_sb", [128, DC * T], F32)
        self.AH = max(DC * min(T, 1024) + 4 * min(T, 1024) + 2 * DC * 256
                      + 2 * HC * 128 + HC * min(T, 1024),
                      3 * DC * T + 2 * DC * 128 + (T // 128) * min(T, 256) + T,
                      52000)
        self.AFW = 4096
        self.arena_h = alloc("arena_h", [128, self.AH], BF16)
        self.arena_f = alloc("arena_f", [128, self.AFW], F32)

        self.ones1k = alloc("ones1k", [128, 128], BF16)
        self.ones4k = alloc("ones4k", [128, 128], BF16)
        self.identh = alloc("identh", [128, 128], BF16)
        self.onesc = alloc("onesc", [1, 128], BF16)
        self.onescf = alloc("onescf", [1, 128], F32)
        self.onescol = alloc("onescol", [128, 1], F32)

        bspec = [("l0_bq", DC), ("l0_bk", DC), ("l0_g1", DC), ("l0_b1v", DC),
                 ("l0_g2", HC), ("l0_b2v", HC), ("l0_fb1", HC), ("l0_fb2h", DC),
                 ("l1_bdown", LC), ("l1_bout", DC), ("l1_fb1", HC), ("l1_fb2h", DC),
                 ("m1", 1), ("m1h", 1), ("m2", 1), ("m2h", 1), ("scr", 8)]
        self.bofs, _bo = {}, 0
        for nm, w in bspec:
            self.bofs[nm] = _bo
            _bo += w
        self.BW = _bo
        self.bias_sb = alloc("bias_sb", [128, self.BW], F32)

        self.bvbc_sb = alloc("bvbc_sb", [128, D], BF16)
        self.bqbc_sb = alloc("bqbc_sb", [128, D], BF16)
        self.bupbc_sb = alloc("bupbc_sb", [128, 2 * D], BF16)
        self.rope_sn = alloc("rope_sn", [128, TC * 64], BF16)
        self.rope_cs = alloc("rope_cs", [128, TC * 64], BF16)

        self.pb = [nc.alloc_psum_tensor(f"pb{i}", [128, 512], F32) for i in range(7)]
        self.ptr = nc.alloc_psum_tensor("ptr", [128, 1024], BF16)

        self.s_ld = self.sem("s_ld")
        self.s_wa = [self.sem(f"s_wa{i}") for i in range(2)]
        self.s_ca = [self.sem(f"s_ca{i}") for i in range(2)]
        self.s_fr = [self.sem(f"s_fr{i}") for i in range(2)]
        self.s_mm = self.sem("s_mm")
        self.s_ev = self.sem("s_ev")
        self.s_v = self.sem("s_v")
        self.s_g = self.sem("s_g")
        self.s_st = self.sem("s_st")
        self.s_sq = [self.sem("s_sq0"), self.sem("s_sq1")]
        self.s_wc = self.sem("s_wc")

    def hv(self, off, p, dims):
        return AP(self.arena_h, off, [[self.AH, p]] + dims)

    def fv(self, off, p, dims):
        return AP(self.arena_f, off, [[self.AFW, p]] + dims)

    def xv(self, ch, t0, n):
        return AP(self.x_sb, ch * self.T + t0, [[DC * self.T, 128], [1, n]])

    def bcol(self, nm, i=0, n=1):
        return AP(self.bias_sb, self.bofs[nm] + i, [[self.BW, 128], [1, n]])

    def emit_setup(self):
        c, T = self, self.T
        TC = T // 128
        c.dma("sync", AP(self.x_sb, 0, [[DC * T, 128], [T, DC], [1, T]]),
              AP(self.x_t, 0, [[T, 128], [128 * T, DC], [1, T]]), self.s_ld)
        for nm, key in [("l0_bq", "l0_bq"), ("l0_bk", "l0_bk"), ("l0_g1", "l0_g1"),
                        ("l0_b1v", "l0_b1v"), ("l0_g2", "l0_g2"), ("l0_b2v", "l0_b2v"),
                        ("l0_fb1", "l0_fb1"), ("l0_fb2h", "l0_fb2"),
                        ("l1_bdown", "l1_bdown"), ("l1_bout", "l1_bout"),
                        ("l1_fb1", "l1_fb1"), ("l1_fb2h", "l1_fb2")]:
            t = getattr(self, key)
            c.dma("sync", self.bcol(nm, 0, t.shape[1]), t.ap(), self.s_ld)
        ld0 = c.count[self.s_ld]
        c.dma("gpsimd", self.bvbc_sb[:], self.l0_bv_bc.ap(), self.s_ca[0])
        c.dma("gpsimd", self.bqbc_sb[:], self.l1_bq_bc.ap(), self.s_ca[0])
        c.dma("gpsimd", self.bupbc_sb[:], self.l1_bup_bc.ap(), self.s_ca[0])

        c.op("gpsimd", "memset", self.ones1k[:], 1.0 / 1024.0, drain=False)
        c.op("gpsimd", "memset", self.ones4k[:], 1.0 / 4096.0, drain=False)
        c.op("gpsimd", "memset", self.onesc[:], 1.0, drain=False)
        c.op("gpsimd", "memset", self.onescf[:], 1.0, drain=False)
        c.op("gpsimd", "memset", self.onescol[:], 1.0, drain=False)
        c.op("gpsimd", "memset", self.identh[:], 1.0)
        c.op("gpsimd", "affine_select", self.identh[:], self.identh[:], [[1, 128]],
             ALU.is_equal, 0.0, base=0, channel_multiplier=-1)
        c.dma("gpsimd", self.rope_sn[:], self.rope_snh.ap(), self.s_ca[0])
        c.dma("gpsimd", self.rope_cs[:], self.rope_csh.ap(), self.s_ca[0])

        c.wait("vector", self.s_ld, ld0)
        c.op("vector", "tensor_scalar_mul", c.bcol("l0_fb2h", 0, DC),
             c.bcol("l0_fb2h", 0, DC), 0.5)
        c.op("vector", "tensor_scalar_mul", c.bcol("l1_fb2h", 0, DC),
             c.bcol("l1_fb2h", 0, DC), 0.5, inc=self.s_v)
        for raw, cache in [(self.l0_wqt, self.l0_wqc), (self.l0_wkt, self.l0_wkc),
                           (self.l0_wvt, self.l0_wvc), (self.l1_w1, self.l1_w1c),
                           (self.l1_w2, self.l1_w2c), (self.l1_wq, self.l1_wqc),
                           (self.l1_wdown, self.l1_wdc), (self.l1_wup, self.l1_wuc),
                           (self.l1_wout, self.l1_woc)]:
            R, Cc = raw.shape
            c.dma("gpsimd", AP(cache, 0, [[Cc, R], [1, Cc]]),
                  AP(raw, 0, [[Cc, R], [1, Cc]]), self.s_wc)
        self.wc_done = [(self.s_wc, self.count[self.s_wc])]
        for e in ("vector", "gpsimd", "scalar", "tensor"):
            c.wait(e, self.s_ca[0])
        c.barrier(tag="setup")

    def emit_wcache(self, pairs):
        c = self
        for p in range(2):
            c.wait("vector", self.s_sq[p])
            c.wait("sync", self.s_sq[p])
        stq = {}
        it = 0
        for raw, cache in pairs:
            R, C = raw.shape
            W = min(C, 2048)
            nsl = C // W
            for r0 in range(0, R, 128):
                for sl in range(nsl):
                    par = it % 2
                    wsl = c.fv(par * 2048, 128, [[1, W]])
                    hsl = c.hv(par * 2048, 128, [[1, W]])
                    if it >= 2:
                        c.wait("sync", self.s_v, stq[it - 2][0])
                    lt = c.dma("sync", wsl,
                               AP(raw, r0 * C + sl * W, [[C, 128], [1, W]]),
                               self.s_wa[par])
                    c.wait("vector", self.s_wa[par], lt)
                    if it >= 2:
                        c.wait("vector", self.s_sq[par], stq[it - 2][1])
                    vt = c.op("vector", "tensor_copy", hsl, wsl, inc=self.s_v)
                    c.wait("sync", self.s_v, vt)
                    st = c.dma("sync",
                               AP(cache, r0 * C + sl * W, [[C, 128], [1, W]]),
                               hsl, self.s_sq[par])
                    stq[it] = (vt, st)
                    it += 1
        self.wc_done = [(self.s_sq[p], self.count[self.s_sq[p]]) for p in range(2)]
        for p in range(2):
            c.wait("sync", self.s_sq[p])
        c.barrier(tag="wcache")

    def emit_quant(self):
        c = self
        NW = 2048
        for raw, cache, mcol, mhcol in [(self.t1_raw, self.t1c, "m1", "m1h"),
                                        (self.t2_raw, self.t2c, "m2", "m2h")]:
            nout = raw.shape[1]
            nic = raw.shape[0] // 128
            nsl = max(1, nout // NW)
            W = min(nout, NW)
            partial = c.bcol("scr", 0)
            c.wait("sync", self.s_v)
            red_t = {}
            it = 0
            for ic in range(nic):
                for sl in range(nsl):
                    par = it % 2
                    wsl = c.fv(par * 2048, 128, [[1, W]])
                    if it >= 2:
                        c.wait("sync", self.s_v, red_t[it - 2])
                    lt = c.dma("sync", wsl,
                               AP(raw, ic * 128 * nout + sl * W, [[nout, 128], [1, W]]),
                               self.s_wa[par])
                    c.wait("vector", self.s_wa[par], lt)
                    red = c.bcol("scr", 1)
                    c.op("vector", "tensor_reduce", red, wsl, AX, ALU.add,
                         apply_absolute_value=True)
                    if it == 0:
                        red_t[it] = c.op("vector", "tensor_copy", partial, red,
                                         inc=self.s_v)
                    else:
                        red_t[it] = c.op("vector", "tensor_tensor", partial,
                                         partial, red, ALU.add, inc=self.s_v)
                    it += 1
            c.wait("tensor", self.s_v)
            tot = self.pb[6]
            c.mm(tot[0:1, 0:1], partial, self.onescol[:], True, True, inc=self.s_mm)
            c.wait("vector", self.s_mm)
            m1p = AP(self.bias_sb, self.bofs["scr"] + 2, [[self.BW, 1], [1, 1]])
            c.op("vector", "tensor_single_scalar", m1p, tot[0:1, 0:1],
                 1.0 / float(nic * 128 * nout), ALU.mult)
            c.op("vector", "tensor_scalar_max", m1p, m1p, 1e-5, inc=self.s_v)
            c.wait("tensor", self.s_v)
            mbcf = self.pb[5][:, 0:1]
            c.mm(mbcf, self.onescf[:, 0:128],
                 AP(self.bias_sb, self.bofs["scr"] + 2, [[self.BW, 1], [1, 1]]),
                 True, True, inc=self.s_mm)
            c.wait("vector", self.s_mm)
            c.op("vector", "tensor_copy", c.bcol(mcol), mbcf)
            c.op("vector", "tensor_scalar_mul", c.bcol(mhcol), mbcf, 0.5, inc=self.s_v)
            c.wait("sync", self.s_v)
            stq = {}
            qt = {}
            it = 0
            for ic in range(nic):
                for sl in range(nsl):
                    par = it % 2
                    wsl = c.fv(par * 2048, 128, [[1, W]])
                    if it >= 2:
                        c.wait("sync", self.s_v, qt[it - 2])
                    lt = c.dma("sync", wsl,
                               AP(raw, ic * 128 * nout + sl * W, [[nout, 128], [1, W]]),
                               self.s_wa[par])
                    tsl = c.hv(par * W, 128, [[1, W]])
                    gtp = c.hv(2 * W + par * W, 128, [[1, W]])
                    c.wait("vector", self.s_wa[par], lt)
                    if it >= 2:
                        c.wait("vector", self.s_sq[par], stq[it - 2])
                    c.op("vector", "tensor_scalar", tsl, wsl, c.bcol(mhcol), None,
                         ALU.is_gt)
                    c.op("vector", "tensor_scalar", gtp, wsl, -1.0, c.bcol(mhcol),
                         ALU.mult, ALU.is_gt)
                    qt[it] = c.op("vector", "tensor_tensor", tsl, tsl, gtp,
                                  ALU.subtract, inc=self.s_v)
                    c.wait("sync", self.s_v, qt[it])
                    stq[it] = c.dma("sync",
                                    AP(cache, ic * 128 * nout + sl * W,
                                       [[nout, 128], [1, W]]),
                                    tsl, self.s_sq[par])
                    it += 1
            for p in range(2):
                c.wait("sync", self.s_sq[p])
        c.barrier(tag="quant")

    def ln_stats(self, get_chunk, nch, ones_w, banks, tw, strips, gates=None):
        c = self
        MT = min(tw, 512)
        nmt = tw // MT
        for i in range(nch):
            par = i % 2
            is_bf, src = get_chunk(i)
            if gates is not None:
                c.wait("scalar", *gates[i])
                c.wait("tensor", *gates[i])
            if is_bf:
                xb = src
            else:
                if i >= 2:
                    c.wait("vector", self.s_mm, self.count[self.s_mm] - 1)
                xb = c.hv(strips + par * tw, 128, [[1, tw]])
                c.op("vector", "tensor_copy", xb, src, inc=self.s_v)
            xq_off = strips + 2 * tw + par * tw
            xq = c.hv(xq_off, 128, [[1, tw]])
            if i >= 2:
                c.wait("scalar", self.s_mm, self.count[self.s_mm] - 1)
            if not is_bf:
                c.wait("scalar", self.s_v)
            c.op("scalar", "activation", xq, xb, AF.Square, bias=0.0, scale=1.0,
                 inc=self.s_ev)
            if not is_bf:
                c.wait("tensor", self.s_v)
            c.wait("tensor", self.s_ev)
            for mt in range(nmt):
                mu_b, es_b = banks[mt]
                kw = dict(start=(i == 0), stop=(i == nch - 1))
                c.mm(self.pb[mu_b][:, 0:MT],  ones_w,
                     AP(xb.tensor, xb.offset + mt * MT, [[xb.ap[0][0], 128], [1, MT]]),
                     **kw)
                c.mm(self.pb[es_b][:, 0:MT], ones_w,
                     c.hv(xq_off + mt * MT, 128, [[1, MT]]),
                     inc=(self.s_mm if mt == nmt - 1 else None), **kw)

    def rstd(self, banks, tw, rs_off, mu_off):
        c = self
        MT = min(tw, 512)
        c.wait("vector", self.s_mm)
        c.wait("vector", self.s_g)
        for mt in range(tw // MT):
            mu_b, es_b = banks[mt]
            ve = c.fv(0, 128, [[1, MT]])
            rs = c.fv(rs_off + mt * MT, 128, [[1, MT]])
            mu_s = c.fv(mu_off + mt * MT, 128, [[1, MT]])
            if mt > 0:
                c.wait("vector", self.s_ev)
            c.op("vector", "tensor_copy", mu_s, self.pb[mu_b][:, 0:MT])
            c.op("vector", "tensor_single_scalar", ve, self.pb[es_b][:, 0:MT],
                 LN_EPS, ALU.add)
            c.op("vector", "tensor_tensor", rs, mu_s, mu_s, ALU.mult)
            c.op("vector", "tensor_tensor", ve, ve, rs, ALU.subtract, inc=self.s_v)
            c.wait("scalar", self.s_v)
            c.op("scalar", "activation", ve, ve, AF.Ln, bias=0.0, scale=1.0)
            c.op("scalar", "activation", rs, ve, AF.Exp, bias=0.0, scale=-0.5,
                 inc=self.s_ev)
        self._rstd_vg = self.count[self.s_v]
        c.wait("vector", self.s_ev)

    def ln_apply(self, dst_off, src_ap_fn, nch, tw, rs_off, mu_off,
                 gain=None, bias=None, stride=None, split=True):
        c = self
        MT = min(tw, 512)
        stride = stride if stride is not None else tw
        gates = []
        c.wait("gpsimd", self.s_ev)
        c.wait("gpsimd", self.s_v)
        c.wait("gpsimd", self.s_mm)
        c.wait("vector", self.s_ev)
        for i in range(nch):
            eng = ("vector", "gpsimd")[i % 2] if split else "vector"
            sem = self.s_v if eng == "vector" else self.s_g
            toff = 0 if eng == "vector" else self.AFW - MT
            for mt in range(tw // MT):
                o = mt * MT
                tmp = c.fv(toff, 128, [[1, MT]])
                dst = c.hv(dst_off + i * stride + o, 128, [[1, MT]])
                c.op(eng, "tensor_tensor", tmp, src_ap_fn(i, o, MT),
                     c.fv(mu_off + o, 128, [[1, MT]]), ALU.subtract)
                if gain is None:
                    c.op(eng, "tensor_tensor", dst, tmp,
                         c.fv(rs_off + o, 128, [[1, MT]]), ALU.mult, inc=sem)
                else:
                    c.op(eng, "tensor_tensor", tmp, tmp,
                         c.fv(rs_off + o, 128, [[1, MT]]), ALU.mult)
                    c.op(eng, "tensor_scalar", dst, tmp, c.bcol(gain, i),
                         c.bcol(bias, i), ALU.mult, ALU.add, inc=sem)
            gates.append((sem, self.count[sem]))
        return gates

    def emit_ffn(self, layer):
        c, T = self, self.T
        TP = min(T, 1024)
        NTP = T // TP
        MT = min(TP, 512)
        NMT = TP // MT
        HU = 0
        HS = HU + DC * TP
        HWA = HS + 4 * TP
        HWB = HWA + 2 * DC * 256
        HG = HWB + 2 * HC * 128
        assert HG + HC * TP <= self.AH
        F_RS = MT
        F_MU = F_RS + TP
        F_TB = F_MU + TP
        assert F_TB + 2 * MT <= self.AFW - MT

        w1 = self.t1c if layer == 0 else self.l1_w1c
        w2 = self.t2c if layer == 0 else self.l1_w2c
        ldsem = self.s_wa
        ldeng = "sync"
        for g in getattr(self, "wc_done", ()):
            c.wait(ldeng, *g)
        fb1 = "l0_fb1" if layer == 0 else "l1_fb1"
        fb2h = "l0_fb2h" if layer == 0 else "l1_fb2h"
        banks = [(0, 1), (2, 3)][:NMT]
        rb = (self.count[self.s_ev], self.count[self.s_v])

        for tp in range(NTP):
            t0 = tp * TP
            self.ln_stats(lambda i: (False, c.xv(i, t0, TP)), DC, self.ones1k[:],
                          banks, TP, HS)
            self.rstd(banks, TP, F_RS, F_MU)
            gu = self.ln_apply(HU, lambda i, o, w: c.xv(i, t0 + o, w), DC, TP,
                               F_RS, F_MU)
            if layer == 0:
                self.ln_stats(lambda i: (True, c.hv(HU + i * TP, 128, [[1, TP]])),
                              DC, self.ones1k[:], banks, TP, HS, gates=gu)
                self.rstd(banks, TP, F_RS, F_MU)
                gu = self.ln_apply(HU,
                                   lambda i, o, w: c.hv(HU + i * TP + o, 128, [[1, w]]),
                                   DC, TP, F_RS, F_MU, gain="l0_g1", bias="l0_b1v")

            nslA = HID // 256

            def mkA(s):
                par = s % 2
                dst = c.hv(HWA + par * DC * 256, 128, [[256, DC], [1, 256]])
                src = AP(w1, s * 256, [[HID, 128], [128 * HID, DC], [1, 256]])
                return dst, src

            ssA = SlabStream(c, ldeng, ldsem, nslA, mkA)
            rot = Rot(c, (4, 5), self.s_ev, bases=rb)
            ev_g = {}
            sq_tgt, sq_guard = {}, {}

            def ems_sq(j):
                if j >= 2:
                    c.wait("scalar", self.s_mm, sq_guard[j - 2])
                c.op("scalar", "activation",
                     c.hv(HS + 2 * TP + (j % 2) * TP, 128, [[1, TP]]),
                     c.hv(HG + j * TP, 128, [[1, TP]]), AF.Square,
                     bias=0.0, scale=1.0, inc=self.s_ev)
                sq_tgt[j] = (self.s_ev, self.count[self.s_ev])

            def ems_mm(j):
                c.wait("tensor", *sq_tgt[j])
                if j == 0:
                    c.wait("tensor", self.s_v, self._rstd_vg)
                for mt in range(NMT):
                    mu_b, es_b = banks[mt]
                    kw = dict(start=(j == 0), stop=(j == HC - 1))
                    c.mm(self.pb[mu_b][:, 0:MT], self.ones4k[:],
                         c.hv(HG + j * TP + mt * MT, 128, [[1, MT]]), **kw)
                    c.mm(self.pb[es_b][:, 0:MT], self.ones4k[:],
                         c.hv(HS + 2 * TP + (j % 2) * TP + mt * MT,
                              128, [[1, MT]]),
                         inc=(self.s_mm if mt == NMT - 1 else None), **kw)
                sq_guard[j] = self.count[self.s_mm]

            for s in range(nslA):
                par = s % 2
                ssA.wait_loaded(s)
                for oc2 in range(2):
                    oc = 2 * s + oc2
                    for mt in range(NMT):
                        b = rot.acquire()
                        for ic in range(DC):
                            c.wait("tensor", *gu[ic])
                            c.mm(self.pb[b][:, 0:MT],
                                 c.hv(HWA + par * DC * 256 + ic * 256 + oc2 * 128,
                                      128, [[1, 128]]),
                                 c.hv(HU + ic * TP + mt * MT, 128, [[1, MT]]),
                                 start=(ic == 0), stop=(ic == DC - 1),
                                 inc=(self.s_mm if ic == DC - 1 else None))
                        c.wait("scalar", self.s_mm)
                        scale = c.bcol("m1") if layer == 0 else 1.0
                        c.op("scalar", "activation",
                             c.hv(HG + oc * TP + mt * MT, 128, [[1, MT]]),
                             self.pb[b][:, 0:MT], AF.Gelu, bias=c.bcol(fb1, oc),
                             scale=scale, inc=self.s_ev)
                        rot.release(b)
                        ev_g[oc] = (self.s_ev, self.count[self.s_ev])
                    if layer == 0:
                        ems_sq(oc)
                        if oc >= 1:
                            ems_mm(oc - 1)
                ssA.done(s, self.s_mm)
            if layer == 0:
                ems_mm(HC - 1)
            rb = (self.count[self.s_ev], self.count[self.s_v])

            if layer == 0:
                self.rstd(banks, TP, F_RS, F_MU)
                gg = self.ln_apply(HG,
                                   lambda i, o, w: c.hv(HG + i * TP + o, 128, [[1, w]]),
                                   HC, TP, F_RS, F_MU, gain="l0_g2", bias="l0_b2v")
            else:
                gg = [ev_g[i] for i in range(HC)]

            nslB = DC

            def mkB(s):
                par = s % 2
                dst = c.hv(HWB + par * HC * 128, 128, [[128, HC], [1, 128]])
                src = AP(w2, s * 128, [[D, 128], [128 * D, HC], [1, 128]])
                return dst, src

            ssB = SlabStream(c, ldeng, ldsem, nslB, mkB)
            rot = Rot(c, (4, 5), self.s_v, bases=rb)
            for s in range(nslB):
                par = s % 2
                ssB.wait_loaded(s)
                ec = s
                for mt in range(NMT):
                    b = rot.acquire()
                    for hc in range(HC):
                        c.wait("tensor", *gg[hc])
                        c.mm(self.pb[b][:, 0:MT],
                             c.hv(HWB + par * HC * 128 + hc * 128, 128, [[1, 128]]),
                             c.hv(HG + hc * TP + mt * MT, 128, [[1, MT]]),
                             start=(hc == 0), stop=(hc == HC - 1),
                             inc=(self.s_mm if hc == HC - 1 else None))
                    c.wait("scalar", self.s_mm)
                    tb = c.fv(F_TB + (rot.i % 2) * MT, 128, [[1, MT]])
                    scale = c.bcol("m2h") if layer == 0 else 0.5
                    c.op("scalar", "mul", tb, self.pb[b][:, 0:MT], scale)
                    c.op("scalar", "add", tb, tb, c.bcol(fb2h, ec), inc=self.s_ev)
                    c.wait("vector", self.s_ev)
                    c.op("vector", "tensor_tensor", c.xv(ec, t0 + mt * MT, MT),
                         c.xv(ec, t0 + mt * MT, MT), tb, ALU.add, inc=self.s_v)
                    rot.release(b)
                ssB.done(s, self.s_mm)
            rb = (self.count[self.s_ev], self.count[self.s_v])

    def emit_attn0(self):
        c, T = self, self.T
        MT = min(T, 512)
        NMT = T // MT
        TC = T // 128
        QW = min(T, 256)
        NQ = T // QW
        KW = min(T, 512)
        NKT = T // KW
        HU = 0
        HK = HU + DC * T
        HV = HK + DC * T
        HWS = HV + TC * 1024
        HET = HWS + 2 * DC * 128
        HE = HET + TC * QW
        assert HE + T <= self.AH, (HE + T, self.AH)
        HPW = min(T, 1024)
        F_RS = MT
        F_MU = F_RS + HPW
        assert F_MU + HPW <= self.AFW

        for hp in range(T // HPW):
            t0 = hp * HPW
            nmt = HPW // MT
            bks = [(0, 1), (2, 3)][:nmt]
            self.ln_stats(lambda i, t0=t0: (False, c.xv(i, t0, HPW)), DC,
                          self.ones1k[:], bks, HPW, HK)
            self.rstd(bks, HPW, F_RS, F_MU)
            self.ln_apply(HU + t0, lambda i, o, w, t0=t0: c.xv(i, t0 + o, w),
                          DC, HPW, F_RS, F_MU, stride=T)
        c.wait("tensor", self.s_v)
        c.wait("tensor", self.s_g)

        def mk_ws(src_t, width):
            def mk(s):
                par = s % 2
                dst = c.hv(HWS + par * DC * 128, 128, [[128, DC], [1, width]])
                src = AP(src_t, s * width, [[D, 128], [128 * D, DC], [1, width]])
                return dst, src
            return mk

        for g in getattr(self, "wc_done", ()):
            c.wait("sync", *g)
        ss = SlabStream(c, "sync", self.s_wa, DC, mk_ws(self.l0_wkc, 128))
        rot = Rot(c, (4, 5), self.s_ev)
        for s in range(DC):
            par = s % 2
            ss.wait_loaded(s)
            oc = s
            for mt in range(NMT):
                b = rot.acquire()
                for ic in range(DC):
                    c.mm(self.pb[b][:, 0:MT],
                         c.hv(HWS + par * DC * 128 + ic * 128, 128, [[1, 128]]),
                         c.hv(HU + ic * T + mt * MT, 128, [[1, MT]]),
                         start=(ic == 0), stop=(ic == DC - 1),
                         inc=(self.s_mm if ic == DC - 1 else None))
                c.wait("scalar", self.s_mm)
                c.op("scalar", "add", c.hv(HK + oc * T + mt * MT, 128, [[1, MT]]),
                     self.pb[b][:, 0:MT], c.bcol("l0_bk", oc), inc=self.s_ev)
                rot.release(b)
            ss.done(s, self.s_mm)
        HVS = HWS
        assert HVS + 2 * DC * 512 <= self.AH

        def mk_vs(s):
            par = s % 2
            dst = c.hv(HVS + par * DC * 512, 128, [[512, DC], [1, 512]])
            src = AP(self.l0_wvc, s * 512, [[D, 128], [128 * D, DC], [1, 512]])
            return dst, src

        ss = SlabStream(c, "sync", self.s_wa, 2, mk_vs)
        rot = Rot(c, (4, 5), self.s_v)
        for s in range(2):
            par = s % 2
            ss.wait_loaded(s)
            for tc in range(TC):
                b = rot.acquire()
                for ic in range(DC):
                    c.mm(self.pb[b][:, 0:512],
                         c.hv(HU + ic * T + tc * 128, 128, [[1, 128]]),
                         c.hv(HVS + par * DC * 512 + ic * 512, 128, [[1, 512]]),
                         start=(ic == 0), stop=(ic == DC - 1),
                         inc=(self.s_mm if ic == DC - 1 else None))
                c.wait("vector", self.s_mm)
                c.op("vector", "tensor_tensor",
                     c.hv(HV + tc * 1024 + s * 512, 128, [[1, 512]]),
                     self.pb[b][:, 0:512],
                     AP(self.bvbc_sb, s * 512, [[D, 128], [1, 512]]),
                     ALU.add, inc=self.s_v)
                rot.release(b)
            ss.done(s, self.s_mm)
        c.wait("tensor", self.s_v)
        c.wait("scalar", self.s_v)

        QT = min(T, 256)
        NQT = T // QT
        mkq = mk_ws(self.l0_wqc, 128)
        qev = {}
        for qt in range(NQT):
            ss = SlabStream(c, "sync", self.s_wa, DC, mkq)
            for oc in range(DC):
                s = oc
                par = s % 2
                ss.wait_loaded(s)
                if qt > 0:
                    c.wait("tensor", *qev[oc])
                for ic in range(DC):
                    c.mm(self.pb[oc // 2][:, (oc % 2) * QT:(oc % 2) * QT + QT],
                         c.hv(HWS + par * DC * 128 + ic * 128, 128, [[1, 128]]),
                         c.hv(HU + ic * T + qt * QT, 128, [[1, QT]]),
                         start=(ic == 0), stop=(ic == DC - 1),
                         inc=(self.s_mm if ic == DC - 1 else None))
                ss.done(s, self.s_mm)
            c.wait("scalar", self.s_mm)
            for oc in range(DC):
                c.op("scalar", "add",
                     c.hv(HU + oc * T + qt * QT, 128, [[1, QT]]),
                     self.pb[oc // 2][:, (oc % 2) * QT:(oc % 2) * QT + QT],
                     c.bcol("l0_bq", oc), inc=self.s_ev)
                qev[oc] = (self.s_ev, self.count[self.s_ev])
        c.wait("tensor", self.s_ev)

        ag = {}
        rotAV = Rot(c, (4, 5), self.s_v)

        def emit_T(gq):
            qc = gq % 2
            heb = HE if gq % 2 == 0 else HWS
            c.wait("tensor", self.s_v, ag[gq]["p"])
            for kc8 in range((TC + 7) // 8):
                n8 = min(8, TC - kc8 * 8)
                for k8 in range(n8):
                    kc = kc8 * 8 + k8
                    c.op("tensor", "transpose",
                         self.ptr[:, k8 * 128:(k8 + 1) * 128],
                         c.hv(heb + kc * 128, 128, [[1, 128]]),
                         self.identh[:],
                         inc=(self.s_mm if k8 == n8 - 1 else None), drain=False)
                c.wait("vector", self.s_mm)
                for k8 in range(n8):
                    kc = kc8 * 8 + k8
                    c.op("vector", "tensor_copy",
                         c.hv(HET + kc * QW + qc * 128, 128, [[1, 128]]),
                         self.ptr[:, k8 * 128:(k8 + 1) * 128],
                         inc=(self.s_v if k8 == n8 - 1 else None))
                c.wait("tensor", self.s_v)
            ag[gq]["T"] = self.count[self.s_mm]

        def emit_AV(qs):
            for ec in range(DC):
                b = rotAV.acquire()
                for kc in range(TC):
                    c.mm(self.pb[b][:, 0:QW],
                         c.hv(HV + kc * 1024 + ec * 128, 128, [[1, 128]]),
                         c.hv(HET + kc * QW, 128, [[1, QW]]),
                         start=(kc == 0), stop=(kc == TC - 1),
                         inc=(self.s_mm if kc == TC - 1 else None))
                c.wait("vector", self.s_mm)
                c.op("vector", "tensor_tensor", c.xv(ec, qs * QW, QW),
                     c.xv(ec, qs * QW, QW), self.pb[b][:, 0:QW], ALU.add,
                     inc=self.s_v)
                rotAV.release(b)

        NGQ = T // 128
        for gq in range(NGQ):
            ag[gq] = {}
            q0 = gq * 128
            if gq > 0:
                c.wait("tensor", self.s_v, ag[gq - 1]["mx"])
                c.wait("tensor", self.s_ev, ag[gq - 1]["exp"])
            for kt in range(NKT):
                for ic in range(DC):
                    c.mm(self.pb[kt][:, 0:KW],
                         c.hv(HU + ic * T + q0, 128, [[1, 128]]),
                         c.hv(HK + ic * T + kt * KW, 128, [[1, KW]]),
                         start=(ic == 0), stop=(ic == DC - 1),
                         inc=(self.s_mm if ic == DC - 1 and kt == NKT - 1
                              else None))
            ag[gq]["s"] = self.count[self.s_mm]
            if gq > 0:
                emit_T(gq - 1)
            if gq % 2 == 0 and gq >= 2:
                emit_AV((gq - 2) // 2)
            c.wait("vector", self.s_mm, ag[gq]["s"])
            for kt in range(NKT):
                c.op("vector", "tensor_reduce", c.bcol("scr", kt),
                     self.pb[kt][:, 0:KW], AX, ALU.max)
            c.op("vector", "tensor_reduce", c.bcol("scr", 4),
                 c.bcol("scr", 0, NKT), AX, ALU.max)
            ag[gq]["mx"] = c.op("vector", "tensor_single_scalar", c.bcol("scr", 4),
                                c.bcol("scr", 4), -1.0 / 32.0, ALU.mult,
                                inc=self.s_v)
            c.wait("scalar", self.s_v, ag[gq]["mx"])
            heb = HE if gq % 2 == 0 else HWS
            if gq > 1:
                c.wait("scalar", self.s_mm, ag[gq - 2]["T"])
            for kt in range(NKT):
                c.op("scalar", "activation", c.hv(heb + kt * KW, 128, [[1, KW]]),
                     self.pb[kt][:, 0:KW], AF.Exp, bias=c.bcol("scr", 4),
                     scale=1.0 / 32.0, accum_out=c.bcol("scr", kt),
                     inc=self.s_ev)
            ag[gq]["exp"] = self.count[self.s_ev]
            c.wait("vector", self.s_ev, ag[gq]["exp"])
            ssum = c.bcol("scr", 5)
            c.op("vector", "tensor_reduce", ssum, c.bcol("scr", 0, NKT), AX,
                 ALU.add)
            c.op("vector", "reciprocal", ssum, ssum)
            ag[gq]["p"] = c.op("vector", "tensor_scalar_mul",
                               c.hv(heb, 128, [[1, T]]),
                               c.hv(heb, 128, [[1, T]]), ssum, inc=self.s_v)
        emit_T(NGQ - 1)
        emit_AV((NGQ - 2) // 2)
        c.wait("tensor", self.s_v)

    def emit_attn1(self):
        c, T = self, self.T
        MT = min(T, 512)
        NMT = T // MT
        TC = T // 128
        RN = TC * 64
        HU = 0
        HWQ = HU + DC * T
        HWU = HWQ + DC * 1024
        HWD = HWU + LC * 2048
        HKL = HWD + DC * 256
        HQC = HKL + LC * T
        HKV = HQC + 1024
        HQR = HKV + 2048
        HKR = HQR + 1024
        HPR = HKR + 1024
        HP = HPR + 4096
        HOC = HP + 256
        HOF = HOC + 2 * 1024
        HWO = HOF + DC * 512
        HPR2 = HWO + 2 * DC * 128
        assert HPR2 + 4096 <= self.AH, (HPR2 + 4096, self.AH)
        HPW = min(T, 1024)
        F_RS = MT
        F_MU = F_RS + HPW
        F_S = F_MU + HPW
        F_S2 = F_S + 256
        assert F_S2 + 256 <= self.AFW

        for hp in range(T // HPW):
            t0 = hp * HPW
            nmt = HPW // MT
            bks = [(0, 1), (2, 3)][:nmt]
            self.ln_stats(lambda i, t0=t0: (False, c.xv(i, t0, HPW)), DC,
                          self.ones1k[:], bks, HPW, HWQ)
            self.rstd(bks, HPW, F_RS, F_MU)
            self.ln_apply(HU + t0, lambda i, o, w, t0=t0: c.xv(i, t0 + o, w),
                          DC, HPW, F_RS, F_MU, stride=T)
        c.wait("tensor", self.s_v)
        c.wait("tensor", self.s_g)
        c.wait("gpsimd", self.s_v)

        for g in getattr(self, "wc_done", ()):
            c.wait("sync", *g)
        c.wait("sync", self.s_mm)
        c.dma("sync", c.hv(HWQ, 128, [[1024, DC], [1, 1024]]),
              AP(self.l1_wqc, 0, [[D, 128], [128 * D, DC], [1, D]]), self.s_wa[0])
        c.dma("sync", c.hv(HWU, 128, [[2048, LC], [1, 2048]]),
              AP(self.l1_wuc, 0, [[2 * D, 128], [128 * 2 * D, LC], [1, 2 * D]]),
              self.s_wa[0])
        c.dma("sync", c.hv(HWD, 128, [[256, DC], [1, 256]]),
              AP(self.l1_wdc, 0, [[DL, 128], [128 * DL, DC], [1, DL]]),
              self.s_wa[0])
        tg_w = self.count[self.s_wa[0]]
        c.wait("tensor", self.s_wa[0], tg_w)

        rot = Rot(c, (4, 5), self.s_ev)
        for lc in range(LC):
            for mt in range(NMT):
                b = rot.acquire()
                for ic in range(DC):
                    c.mm(self.pb[b][:, 0:MT],
                         c.hv(HWD + ic * 256 + lc * 128, 128, [[1, 128]]),
                         c.hv(HU + ic * T + mt * MT, 128, [[1, MT]]),
                         start=(ic == 0), stop=(ic == DC - 1),
                         inc=(self.s_mm if ic == DC - 1 else None))
                c.wait("scalar", self.s_mm)
                c.op("scalar", "add",
                     c.hv(HKL + lc * T + mt * MT, 128, [[1, MT]]),
                     self.pb[b][:, 0:MT], c.bcol("l1_bdown", lc), inc=self.s_ev)
                rot.release(b)
        c.wait("tensor", self.s_ev)

        gates = {}
        self._a1rb = (self.count[self.s_ev], self.count[self.s_v])

        def emit_proj(tc):
            rot = Rot(c, (4, 5), self.s_v, bases=self._a1rb)
            for ot in range(2):
                b = rot.acquire()
                for ic in range(DC):
                    c.mm(self.pb[b][:, 0:512],
                         c.hv(HU + ic * T + tc * 128, 128, [[1, 128]]),
                         c.hv(HWQ + ic * 1024 + ot * 512, 128, [[1, 512]]),
                         start=(ic == 0), stop=(ic == DC - 1),
                         inc=(self.s_mm if ic == DC - 1 else None))
                c.wait("vector", self.s_mm)
                c.op("vector", "tensor_tensor", c.hv(HQC + ot * 512, 128, [[1, 512]]),
                     self.pb[b][:, 0:512],
                     AP(self.bqbc_sb, ot * 512, [[D, 128], [1, 512]]), ALU.add,
                     inc=self.s_v)
                rot.release(b)
            if tc > 0:
                c.wait("vector", *gates[tc - 1]["avg"])
            for ot in range(4):
                b = rot.acquire()
                for lc in range(LC):
                    c.mm(self.pb[b][:, 0:512],
                         c.hv(HKL + lc * T + tc * 128, 128, [[1, 128]]),
                         c.hv(HWU + lc * 2048 + ot * 512, 128, [[1, 512]]),
                         start=(lc == 0), stop=(lc == LC - 1),
                         inc=(self.s_mm if lc == LC - 1 else None))
                c.wait("vector", self.s_mm)
                c.op("vector", "tensor_tensor", c.hv(HKV + ot * 512, 128, [[1, 512]]),
                     self.pb[b][:, 0:512],
                     AP(self.bupbc_sb, ot * 512, [[2 * D, 128], [1, 512]]), ALU.add,
                     inc=self.s_v)
                rot.release(b)
            self._a1rb = (self.count[self.s_ev], self.count[self.s_v])

        def elem_qk(tc, hq, eng, scr):
            sem = self.s_v if eng == "vector" else self.s_g
            qrep = AP(self.arena_h, HQR + hq * 256,
                      [[self.AH, 128], [64, 4], [0, NH], [1, 64]])
            krep = AP(self.arena_h, HKR,
                      [[self.AH, 128], [0, 4], [64, NH], [1, 64]])
            c.op(eng, "tensor_tensor", c.hv(scr, 128, [[1, 4096]]),
                 qrep, krep, ALU.mult)
            for half in (32, 16, 8, 4, 2, 1):
                a = AP(self.arena_h, scr, [[self.AH, 128], [64, 64], [1, half]])
                bb = AP(self.arena_h, scr + half,
                        [[self.AH, 128], [64, 64], [1, half]])
                c.op(eng, "tensor_tensor", a, a, bb, ALU.add)
            return (sem, c.op(eng, "tensor_copy",
                              c.fv(F_S + hq * 64, 128, [[1, 64]]),
                              AP(self.arena_h, scr, [[self.AH, 128], [64, 64]]),
                              inc=sem))

        def elem_av(tc, hq, eng, scr, hoc):
            sem = self.s_v if eng == "vector" else self.s_g
            prep = AP(self.arena_h, HP + hq * 64,
                      [[self.AH, 128], [16, 4], [1, NH], [0, 64]])
            vrep = AP(self.arena_h, HKV + 64,
                      [[self.AH, 128], [0, 4], [128, NH], [1, 64]])
            c.op(eng, "tensor_tensor", c.hv(scr, 128, [[1, 4096]]),
                 prep, vrep, ALU.mult)
            for halfg in (8, 4, 2, 1):
                a = AP(self.arena_h, scr,
                       [[self.AH, 128], [1024, 4], [64, halfg], [1, 64]])
                bb = AP(self.arena_h, scr + halfg * 64,
                        [[self.AH, 128], [1024, 4], [64, halfg], [1, 64]])
                c.op(eng, "tensor_tensor", a, a, bb, ALU.add)
            return (sem, c.op(eng, "tensor_copy",
                              c.hv(hoc + hq * 256, 128, [[64, 4], [1, 64]]),
                              AP(self.arena_h, scr,
                                 [[self.AH, 128], [1024, 4], [1, 64]]),
                              inc=sem))

        def emit_elem(tc):
            g = gates[tc] = {}
            hoc = HOC + (tc % 2) * 1024
            if tc > 0:
                c.wait("vector", *gates[tc - 1]["qkg"])
            ro = tc * 64
            cs_rep = AP(self.rope_cs, ro, [[RN, 128], [0, NH], [1, 64]])
            sn_rep = AP(self.rope_sn, ro, [[RN, 128], [0, NH], [1, 64]])
            q3 = c.hv(HQC, 128, [[64, NH], [1, 64]])
            qsw = AP(self.arena_h, HQC + 32,
                     [[self.AH, 128], [64, NH], [-32, 2], [1, 32]])
            c.op("vector", "tensor_tensor", c.hv(HQR, 128, [[64, NH], [1, 64]]),
                 q3, cs_rep, ALU.mult)
            c.op("vector", "tensor_tensor", c.hv(HPR2, 128, [[64, NH], [1, 64]]),
                 qsw, sn_rep, ALU.mult)
            c.op("vector", "tensor_tensor", c.hv(HQR, 128, [[1, 1024]]),
                 c.hv(HQR, 128, [[1, 1024]]), c.hv(HPR2, 128, [[1, 1024]]), ALU.add)
            k3 = c.hv(HKV, 128, [[128, NH], [1, 64]])
            ksw = AP(self.arena_h, HKV + 32,
                     [[self.AH, 128], [128, NH], [-32, 2], [1, 32]])
            c.op("vector", "tensor_tensor", c.hv(HKR, 128, [[64, NH], [1, 64]]),
                 k3, cs_rep, ALU.mult)
            c.op("vector", "tensor_tensor", c.hv(HPR2, 128, [[64, NH], [1, 64]]),
                 ksw, sn_rep, ALU.mult)
            c.op("vector", "tensor_tensor", c.hv(HKR, 128, [[1, 1024]]),
                 c.hv(HKR, 128, [[1, 1024]]), c.hv(HPR2, 128, [[1, 1024]]), ALU.add,
                 inc=self.s_v)
            rope_t = self.count[self.s_v]
            c.wait("gpsimd", self.s_v, rope_t)
            if tc > 0:
                c.wait("vector", self.s_ev, gates[tc - 1]["exp"])
                c.wait("gpsimd", self.s_ev, gates[tc - 1]["exp"])
                c.wait("gpsimd", self.s_v, gates[tc - 1]["p"])
            elem_qk(tc, 0, "vector", HPR2)
            elem_qk(tc, 1, "vector", HPR2)
            elem_qk(tc, 2, "vector", HPR2)
            g["qkg"] = elem_qk(tc, 3, "vector", HPR2)
            qkv_t = self.count[self.s_v]
            c.wait("vector", *g["qkg"])
            smax = c.fv(F_S2, 128, [[1, NH]])
            c.op("vector", "tensor_reduce", smax,
                 c.fv(F_S, 128, [[16, NH], [1, NH]]), AX, ALU.max, negate=True)
            c.op("vector", "tensor_tensor", c.fv(F_S, 128, [[16, NH], [1, NH]]),
                 c.fv(F_S, 128, [[16, NH], [1, NH]]),
                 AP(self.arena_f, F_S2, [[self.AFW, 128], [1, NH], [0, NH]]),
                 ALU.add, inc=self.s_v)
            c.wait("scalar", self.s_v)
            g["exp"] = c.op("scalar", "activation", c.fv(F_S, 128, [[1, 256]]),
                            c.fv(F_S, 128, [[1, 256]]), AF.Exp, bias=0.0,
                            scale=1.0 / 8.0, inc=self.s_ev)
            c.wait("vector", self.s_ev, g["exp"])
            ssum = c.fv(F_S2, 128, [[1, NH]])
            c.op("vector", "tensor_reduce", ssum,
                 c.fv(F_S, 128, [[16, NH], [1, NH]]), AX, ALU.add)
            c.op("vector", "reciprocal", ssum, ssum)
            g["p"] = c.op("vector", "tensor_tensor",
                          c.hv(HP, 128, [[16, NH], [1, NH]]),
                          c.fv(F_S, 128, [[16, NH], [1, NH]]),
                          AP(self.arena_f, F_S2, [[self.AFW, 128], [1, NH], [0, NH]]),
                          ALU.mult, inc=self.s_v)
            c.wait("gpsimd", self.s_v, g["p"])
            if tc > 1:
                c.wait("vector", self.s_mm, gates[tc - 2]["otr"])
                c.wait("gpsimd", self.s_mm, gates[tc - 2]["otr"])
            elem_av(tc, 0, "vector", HPR2, hoc)
            elem_av(tc, 1, "vector", HPR2, hoc)
            elem_av(tc, 2, "vector", HPR2, hoc)
            g["avv"] = g["avg"] = elem_av(tc, 3, "vector", HPR2, hoc)

        def emit_otrans(tc):
            g = gates[tc]
            hoc = HOC + (tc % 2) * 1024
            slab = tc % 4
            c.wait("tensor", *g["avg"])
            c.wait("tensor", *g["avv"])
            if tc > 0:
                c.wait("tensor", self.s_v, gates[tc - 1]["ofc"])
            for i8 in range(8):
                c.op("tensor", "transpose", self.ptr[:, i8 * 128:(i8 + 1) * 128],
                     c.hv(hoc + i8 * 128, 128, [[1, 128]]), self.identh[:],
                     inc=(self.s_mm if i8 == 7 else None), drain=False)
            g["otr"] = self.count[self.s_mm]
            c.wait("vector", self.s_mm, g["otr"])
            if tc >= 4 and slab == 0:
                c.wait("vector", self.s_mm, gates[tc - 1]["opr"])
            for i8 in range(8):
                c.op("vector", "tensor_copy",
                     c.hv(HOF + i8 * 512 + slab * 128, 128, [[1, 128]]),
                     self.ptr[:, i8 * 128:(i8 + 1) * 128],
                     inc=(self.s_v if i8 == 7 else None))
            g["ofc"] = self.count[self.s_v]

            if slab == 3 or tc == TC - 1:
                nsl = slab + 1
                t0 = (tc - slab) * 128

                def mkO(s):
                    par = s % 2
                    dst = c.hv(HWO + par * DC * 128, 128, [[128, DC], [1, 128]])
                    src = AP(self.l1_woc, s * 128,
                             [[D, 128], [128 * D, DC], [1, 128]])
                    return dst, src

                ssO = SlabStream(c, "sync", self.s_wa, DC, mkO)
                rotO = Rot(c, (4, 5), self.s_v, bases=self._a1rb)
                c.wait("tensor", self.s_v, g["ofc"])
                for s in range(DC):
                    par = s % 2
                    ec = s
                    ssO.wait_loaded(s)
                    b = rotO.acquire()
                    for ic in range(DC):
                        c.mm(self.pb[b][:, 0:nsl * 128],
                             c.hv(HWO + par * DC * 128 + ic * 128, 128, [[1, 128]]),
                             c.hv(HOF + ic * 512, 128, [[1, nsl * 128]]),
                             start=(ic == 0), stop=(ic == DC - 1),
                             inc=(self.s_mm if ic == DC - 1 else None))
                    c.wait("vector", self.s_mm)
                    c.op("vector", "scalar_tensor_tensor", c.xv(ec, t0, nsl * 128),
                         self.pb[b][:, 0:nsl * 128], c.bcol("l1_bout", ec),
                         c.xv(ec, t0, nsl * 128), ALU.add, ALU.add, inc=self.s_v)
                    rotO.release(b)
                    ssO.done(s, self.s_mm)
                g["opr"] = self.count[self.s_mm]
                self._a1rb = (self.count[self.s_ev], self.count[self.s_v])
            else:
                g["opr"] = gates[tc - 1]["opr"] if tc > 0 else 0

        for tc in range(TC):
            emit_proj(tc)
            if tc > 0:
                emit_otrans(tc - 1)
            emit_elem(tc)
        emit_otrans(TC - 1)
        c.wait("tensor", self.s_v)


def build(T=2048, phases=("f0", "a0", "f0", "f1", "a1", "f1"), taps=()):
    b = Builder(T, phases, taps)
    b.declare()
    b.emit_setup()
    if "f0" in phases:
        b.emit_quant()
    for pi, ph in enumerate(phases):
        {"f0": lambda: b.emit_ffn(0),
         "a0": b.emit_attn0,
         "f1": lambda: b.emit_ffn(1),
         "a1": b.emit_attn1}[ph]()
        b.barrier(tag=f"ph{pi}")
        if pi in b.tap_t:
            T_ = b.T
            b.dma("sync", AP(b.tap_t[pi], 0, [[T_, 128], [128 * T_, DC], [1, T_]]),
                  AP(b.x_sb, 0, [[DC * T_, 128], [T_, DC], [1, T_]]), b.s_st)
            b.wait("sync", b.s_st)
            b.barrier(tag=f"tap{pi}")
    T_ = b.T
    b.dma("sync", AP(b.y_t, 0, [[T_, 128], [128 * T_, DC], [1, T_]]),
          AP(b.x_sb, 0, [[DC * T_, 128], [T_, DC], [1, T_]]), b.s_st)
    b.wait("sync", b.s_st)
    return b.finalize()


def _stage(inputs):
    asc = np.ascontiguousarray
    x = np.asarray(inputs["L0_attn_x"], np.float32)
    ncore = x.shape[0]

    def rep(v):
        v = np.asarray(v, np.float32)
        return asc(np.broadcast_to(v[None, :], (128, v.shape[0])))

    def col(v, w):
        return asc(np.asarray(v, np.float32).reshape(w, 128).T)

    shared = {
        "l0_wqt": asc(np.asarray(inputs["L0_attn_Wq"], np.float32).T),
        "l0_wkt": asc(np.asarray(inputs["L0_attn_Wk"], np.float32).T),
        "l0_wvt": asc(np.asarray(inputs["L0_attn_Wv"], np.float32).T),
        "l0_bq": col(inputs["L0_attn_bq"], DC),
        "l0_bk": col(inputs["L0_attn_bk"], DC),
        "l0_bv_bc": rep(inputs["L0_attn_bv"]),
        "t1_raw": asc(np.asarray(inputs["L0_ffn_w1"], np.float32).T),
        "t2_raw": asc(np.asarray(inputs["L0_ffn_w2"], np.float32).T),
        "l0_g1": col(inputs["L0_ffn_ln1_g"], DC),
        "l0_b1v": col(inputs["L0_ffn_ln1_b"], DC),
        "l0_g2": col(inputs["L0_ffn_ln2_g"], HC),
        "l0_b2v": col(inputs["L0_ffn_ln2_b"], HC),
        "l0_fb1": col(inputs["L0_ffn_b1"], HC),
        "l0_fb2": col(inputs["L0_ffn_b2"], DC),
        "l1_wq": asc(np.asarray(inputs["L1_attn_Wq"], np.float32)),
        "l1_bq_bc": rep(inputs["L1_attn_bq"]),
        "l1_wdown": asc(np.asarray(inputs["L1_attn_Wdown"], np.float32)),
        "l1_bdown": col(inputs["L1_attn_bdown"], LC),
        "l1_wup": asc(np.asarray(inputs["L1_attn_Wup"], np.float32)),
        "l1_bup_bc": rep(inputs["L1_attn_bup"]),
        "l1_wout": asc(np.asarray(inputs["L1_attn_Wout"], np.float32)),
        "l1_bout": col(inputs["L1_attn_bout"], DC),
        "l1_w1": asc(np.asarray(inputs["L1_ffn_w1"], np.float32)),
        "l1_fb1": col(inputs["L1_ffn_b1"], HC),
        "l1_w2": asc(np.asarray(inputs["L1_ffn_w2"], np.float32)),
        "l1_fb2": col(inputs["L1_ffn_b2"], DC),
    }
    T = x.shape[1]
    TC = T // 128
    try:
        import jax
        import jax.numpy as jnp
        with jax.default_device(jax.devices("cpu")[0]):
            inv = 1.0 / 10000.0 ** (jnp.arange(0, DH, 2, dtype=jnp.float32) / DH)
            t = jnp.arange(T, dtype=jnp.float32)
            fr = t[:, None] * inv[None, :]
            snf = np.asarray(jnp.sin(fr), dtype=np.float32)
            csf = np.asarray(jnp.cos(fr), dtype=np.float32)
    except Exception:
        e = np.arange(0, DH, 2, dtype=np.float32) / np.float32(DH)
        invf = (np.float32(1.0) / np.power(np.float32(10000.0), e)).astype(np.float32)
        fr = (np.arange(T, dtype=np.float32)[:, None] * invf[None, :]).astype(np.float32)
        snf = np.sin(fr).astype(np.float32)
        csf = np.cos(fr).astype(np.float32)
    sn_tab = np.zeros((128, TC * 64), np.float32)
    cs_tab = np.zeros((128, TC * 64), np.float32)
    for tc in range(TC):
        rows = snf[tc * 128:(tc + 1) * 128]
        sn_tab[:, tc * 64:tc * 64 + 32] = -rows
        sn_tab[:, tc * 64 + 32:tc * 64 + 64] = rows
        cs_tab[:, tc * 64:tc * 64 + 32] = csf[tc * 128:(tc + 1) * 128]
        cs_tab[:, tc * 64 + 32:tc * 64 + 64] = csf[tc * 128:(tc + 1) * 128]
    shared["rope_snh"] = sn_tab
    shared["rope_csh"] = cs_tab
    in_maps = []
    for i in range(ncore):
        m = dict(shared)
        m["x_t"] = asc(x[i].T)
        in_maps.append(m)
    return in_maps


def kernel(**inputs):
    T = inputs["L0_attn_x"].shape[1]
    in_maps = _stage(inputs)
    nc = build(T=T)
    res = run_bass_kernel_spmd(nc, in_maps, core_ids=list(range(len(in_maps))))
    out = np.stack([np.ascontiguousarray(np.asarray(r["y_t"]).T)
                    for r in res.results], axis=0)
    return out.astype(np.float32)


# version 22 (final)
# speedup vs baseline: 1.0348x; 1.0348x over previous
import numpy as np
from contextlib import ExitStack

import concourse.bass as bass
import concourse.mybir as mybir
from concourse.bass_utils import run_bass_kernel_spmd

F32 = mybir.dt.float32
BF16 = mybir.dt.bfloat16
AF = mybir.ActivationFunctionType
ALU = mybir.AluOpType
AX = mybir.AxisListType.X
AP = bass.AP

D, DC = 1024, 8
HID, HC = 4096, 32
NH, DH = 16, 64
DL, LC = 256, 2
LN_EPS = 1e-5

ENGS = ("sync", "gpsimd", "tensor", "vector", "scalar")


class Rot:

    def __init__(self, b, banks, sem, bases=None):
        self.b, self.banks, self.sem = b, banks, sem
        self.pending = {}
        self.i = 0
        if bases is None:
            bases = (b.count[b.s_ev], b.count[b.s_v])
        b.wait("tensor", b.s_ev, bases[0])
        b.wait("tensor", b.s_v, bases[1])

    def acquire(self):
        bank = self.banks[self.i % len(self.banks)]
        if bank in self.pending:
            self.b.wait("tensor", self.sem, self.pending[bank])
        self.i += 1
        return bank

    def release(self, bank):
        self.pending[bank] = self.b.count[self.sem]


class SlabStream:

    def __init__(self, b, eng, sems, nsl, mk, gate=None):
        self.b, self.eng, self.sems, self.nsl, self.mk = b, eng, sems, nsl, mk
        self.ldt = {}
        gsem, gval = gate if gate else (b.s_mm, b.count[b.s_mm])
        if gval:
            b.wait(eng, gsem, gval)
        for s in range(min(2, nsl)):
            dst, src = mk(s)
            self.ldt[s] = b.dma(eng, dst, src, sems[s % 2])

    def wait_loaded(self, s, eng="tensor"):
        self.b.wait(eng, self.sems[s % 2], self.ldt[s])

    def done(self, s, gate_sem, gate_val=None):
        if s + 2 < self.nsl:
            if gate_val is None:
                gate_val = self.b.count[gate_sem]
            self.b.wait(self.eng, gate_sem, gate_val)
            dst, src = self.mk(s + 2)
            self.ldt[s + 2] = self.b.dma(self.eng, dst, src, self.sems[s % 2])


class Builder:
    def __init__(self, T, phases, taps):
        self.T = T
        self.phases = phases
        self.taps = taps
        self.nc = bass.Bass("TRN2", target_bir_lowering=False)
        self.ops = {e: [] for e in ENGS}
        self.semnames = []
        self.count = {}
        self.waited = {}
        self.nbar = 0

    def sem(self, name):
        assert name not in self.count
        self.semnames.append(name)
        self.count[name] = 0
        return name

    def raw(self, eng, fn):
        self.ops[eng].append(fn)

    def op(self, eng, meth, *args, inc=None, drain=True, **kwargs):
        tgt = None
        if inc is not None:
            self.count[inc] += 1
            tgt = self.count[inc]

        def fn(e, S, meth=meth, args=args, kwargs=kwargs, inc=inc, drain=drain):
            ins = getattr(e, meth)(*args, **kwargs)
            if inc is not None:
                ins.then_inc(S[inc], 1)
            if drain:
                e.drain()
        self.raw(eng, fn)
        return tgt

    def mm(self, out, lhsT, rhs, start, stop, inc=None):
        return self.op("tensor", "matmul", out, lhsT, rhs, start=start,
                       stop=stop, inc=inc, drain=False)

    def dma(self, eng, dst, src, sem):
        self.count[sem] += 16
        tgt = self.count[sem]

        def fn(e, S, dst=dst, src=src, sem=sem):
            e.dma_start(dst, src).then_inc(S[sem], 16)
        self.raw(eng, fn)
        return tgt

    def wait(self, eng, sem, val=None):
        if val is None:
            val = self.count[sem]
        seen = self.waited.setdefault(eng, {})
        if seen.get(sem, -1) >= val:
            return val
        seen[sem] = val

        def fn(e, S, sem=sem, val=val):
            e.wait_ge(S[sem], val)
        self.raw(eng, fn)
        return val

    def barrier(self, engs=ENGS, tag=""):
        self.nbar += 1
        name = self.sem(f"bar{self.nbar}{tag}")
        for e in engs:
            self.op(e, "drain", inc=name, drain=False)
        for e in engs:
            self.wait(e, name, len(engs))

    def finalize(self):
        nc = self.nc
        with ExitStack() as es:
            S = {n: es.enter_context(nc.semaphore(n)) for n in self.semnames}
            with nc.Block() as blk:
                for eng in ENGS:
                    def driver(e, _ops=self.ops[eng], _S=S):
                        for f in _ops:
                            f(e, _S)
                    getattr(blk, eng)(driver)
        return nc

    def declare(self):
        nc, T = self.nc, self.T
        TC = T // 128

        def din(name, shape):
            setattr(self, name, nc.dram_tensor(name, shape, F32, kind="ExternalInput"))

        din("x_t", [D, T])
        din("l0_wqt", [D, D]); din("l0_wkt", [D, D]); din("l0_wvt", [D, D])
        din("l0_bq", [128, DC]); din("l0_bk", [128, DC]); din("l0_bv_bc", [128, D])
        din("t1_raw", [D, HID]); din("t2_raw", [HID, D])
        din("l0_g1", [128, DC]); din("l0_b1v", [128, DC])
        din("l0_g2", [128, HC]); din("l0_b2v", [128, HC])
        din("l0_fb1", [128, HC]); din("l0_fb2", [128, DC])
        din("l1_wq", [D, D]); din("l1_bq_bc", [128, D])
        din("l1_wdown", [D, DL]); din("l1_bdown", [128, LC])
        din("l1_wup", [DL, 2 * D]); din("l1_bup_bc", [128, 2 * D])
        din("l1_wout", [D, D]); din("l1_bout", [128, DC])
        din("l1_w1", [D, HID]); din("l1_fb1", [128, HC])
        din("l1_w2", [HID, D]); din("l1_fb2", [128, DC])
        din("rope_snh", [128, TC * 64]); din("rope_csh", [128, TC * 64])

        self.y_t = nc.dram_tensor("y_t", [D, T], F32, kind="ExternalOutput")
        self.t1c = nc.dram_tensor("t1c", [D, HID], BF16, kind="Internal")
        self.t2c = nc.dram_tensor("t2c", [HID, D], BF16, kind="Internal")
        for nm, shp in [("l0_wqc", [D, D]), ("l0_wkc", [D, D]), ("l0_wvc", [D, D]),
                        ("l1_w1c", [D, HID]), ("l1_w2c", [HID, D]),
                        ("l1_wqc", [D, D]), ("l1_wdc", [D, DL]),
                        ("l1_wuc", [DL, 2 * D]), ("l1_woc", [D, D])]:
            setattr(self, nm, nc.dram_tensor(nm, shp, BF16, kind="Internal"))
        self.tap_t = {i: nc.dram_tensor(f"tap{i}", [D, T], F32, kind="ExternalOutput")
                      for i in self.taps}

        alloc = nc.alloc_sbuf_tensor
        self.x_sb = alloc("x_sb", [128, DC * T], F32)
        self.AH = max(DC * min(T, 1024) + 4 * min(T, 1024) + 2 * DC * 256
                      + 2 * HC * 128 + HC * min(T, 1024),
                      3 * DC * T + 2 * DC * 128 + (T // 128) * min(T, 256) + T,
                      52000)
        self.AFW = 4096
        self.arena_h = alloc("arena_h", [128, self.AH], BF16)
        self.arena_f = alloc("arena_f", [128, self.AFW], F32)

        self.ones1k = alloc("ones1k", [128, 128], BF16)
        self.ones4k = alloc("ones4k", [128, 128], BF16)
        self.identh = alloc("identh", [128, 128], BF16)
        self.onesc = alloc("onesc", [1, 128], BF16)
        self.onescf = alloc("onescf", [1, 128], F32)
        self.onescol = alloc("onescol", [128, 1], F32)

        bspec = [("l0_bq", DC), ("l0_bk", DC), ("l0_g1", DC), ("l0_b1v", DC),
                 ("l0_g2", HC), ("l0_b2v", HC), ("l0_fb1", HC), ("l0_fb2h", DC),
                 ("l1_bdown", LC), ("l1_bout", DC), ("l1_fb1", HC), ("l1_fb2h", DC),
                 ("m1", 1), ("m1h", 1), ("m2", 1), ("m2h", 1), ("scr", 8)]
        self.bofs, _bo = {}, 0
        for nm, w in bspec:
            self.bofs[nm] = _bo
            _bo += w
        self.BW = _bo
        self.bias_sb = alloc("bias_sb", [128, self.BW], F32)

        self.bvbc_sb = alloc("bvbc_sb", [128, D], BF16)
        self.bqbc_sb = alloc("bqbc_sb", [128, D], BF16)
        self.bupbc_sb = alloc("bupbc_sb", [128, 2 * D], BF16)
        self.rope_sn = alloc("rope_sn", [128, TC * 64], BF16)
        self.rope_cs = alloc("rope_cs", [128, TC * 64], BF16)

        self.pb = [nc.alloc_psum_tensor(f"pb{i}", [128, 512], F32) for i in range(7)]
        self.ptr = nc.alloc_psum_tensor("ptr", [128, 1024], BF16)

        self.s_ld = self.sem("s_ld")
        self.s_wa = [self.sem(f"s_wa{i}") for i in range(2)]
        self.s_ca = [self.sem(f"s_ca{i}") for i in range(2)]
        self.s_fr = [self.sem(f"s_fr{i}") for i in range(2)]
        self.s_mm = self.sem("s_mm")
        self.s_ev = self.sem("s_ev")
        self.s_v = self.sem("s_v")
        self.s_g = self.sem("s_g")
        self.s_st = self.sem("s_st")
        self.s_sq = [self.sem("s_sq0"), self.sem("s_sq1")]
        self.s_wc = self.sem("s_wc")

    def hv(self, off, p, dims):
        return AP(self.arena_h, off, [[self.AH, p]] + dims)

    def fv(self, off, p, dims):
        return AP(self.arena_f, off, [[self.AFW, p]] + dims)

    def xv(self, ch, t0, n):
        return AP(self.x_sb, ch * self.T + t0, [[DC * self.T, 128], [1, n]])

    def bcol(self, nm, i=0, n=1):
        return AP(self.bias_sb, self.bofs[nm] + i, [[self.BW, 128], [1, n]])

    def emit_setup(self):
        c, T = self, self.T
        TC = T // 128
        c.dma("sync", AP(self.x_sb, 0, [[DC * T, 128], [T, DC], [1, T]]),
              AP(self.x_t, 0, [[T, 128], [128 * T, DC], [1, T]]), self.s_ld)
        for nm, key in [("l0_bq", "l0_bq"), ("l0_bk", "l0_bk"), ("l0_g1", "l0_g1"),
                        ("l0_b1v", "l0_b1v"), ("l0_g2", "l0_g2"), ("l0_b2v", "l0_b2v"),
                        ("l0_fb1", "l0_fb1"), ("l0_fb2h", "l0_fb2"),
                        ("l1_bdown", "l1_bdown"), ("l1_bout", "l1_bout"),
                        ("l1_fb1", "l1_fb1"), ("l1_fb2h", "l1_fb2")]:
            t = getattr(self, key)
            c.dma("sync", self.bcol(nm, 0, t.shape[1]), t.ap(), self.s_ld)
        ld0 = c.count[self.s_ld]
        c.dma("gpsimd", self.bvbc_sb[:], self.l0_bv_bc.ap(), self.s_ca[0])
        c.dma("gpsimd", self.bqbc_sb[:], self.l1_bq_bc.ap(), self.s_ca[0])
        c.dma("gpsimd", self.bupbc_sb[:], self.l1_bup_bc.ap(), self.s_ca[0])

        c.op("gpsimd", "memset", self.ones1k[:], 1.0 / 1024.0, drain=False)
        c.op("gpsimd", "memset", self.ones4k[:], 1.0 / 4096.0, drain=False)
        c.op("gpsimd", "memset", self.onesc[:], 1.0, drain=False)
        c.op("gpsimd", "memset", self.onescf[:], 1.0, drain=False)
        c.op("gpsimd", "memset", self.onescol[:], 1.0, drain=False)
        c.op("gpsimd", "memset", self.identh[:], 1.0)
        c.op("gpsimd", "affine_select", self.identh[:], self.identh[:], [[1, 128]],
             ALU.is_equal, 0.0, base=0, channel_multiplier=-1)
        c.dma("gpsimd", self.rope_sn[:], self.rope_snh.ap(), self.s_ca[0])
        c.dma("gpsimd", self.rope_cs[:], self.rope_csh.ap(), self.s_ca[0])

        c.wait("vector", self.s_ld, ld0)
        c.op("vector", "tensor_scalar_mul", c.bcol("l0_fb2h", 0, DC),
             c.bcol("l0_fb2h", 0, DC), 0.5)
        c.op("vector", "tensor_scalar_mul", c.bcol("l1_fb2h", 0, DC),
             c.bcol("l1_fb2h", 0, DC), 0.5, inc=self.s_v)
        for raw, cache in [(self.l0_wqt, self.l0_wqc), (self.l0_wkt, self.l0_wkc),
                           (self.l0_wvt, self.l0_wvc), (self.l1_w1, self.l1_w1c),
                           (self.l1_w2, self.l1_w2c), (self.l1_wq, self.l1_wqc),
                           (self.l1_wdown, self.l1_wdc), (self.l1_wup, self.l1_wuc),
                           (self.l1_wout, self.l1_woc)]:
            R, Cc = raw.shape
            c.dma("gpsimd", AP(cache, 0, [[Cc, R], [1, Cc]]),
                  AP(raw, 0, [[Cc, R], [1, Cc]]), self.s_wc)
        self.wc_done = [(self.s_wc, self.count[self.s_wc])]
        for e in ("vector", "gpsimd", "scalar", "tensor"):
            c.wait(e, self.s_ca[0])
        c.barrier(tag="setup")

    def emit_wcache(self, pairs):
        c = self
        for p in range(2):
            c.wait("vector", self.s_sq[p])
            c.wait("sync", self.s_sq[p])
        stq = {}
        it = 0
        for raw, cache in pairs:
            R, C = raw.shape
            W = min(C, 2048)
            nsl = C // W
            for r0 in range(0, R, 128):
                for sl in range(nsl):
                    par = it % 2
                    wsl = c.fv(par * 2048, 128, [[1, W]])
                    hsl = c.hv(par * 2048, 128, [[1, W]])
                    if it >= 2:
                        c.wait("sync", self.s_v, stq[it - 2][0])
                    lt = c.dma("sync", wsl,
                               AP(raw, r0 * C + sl * W, [[C, 128], [1, W]]),
                               self.s_wa[par])
                    c.wait("vector", self.s_wa[par], lt)
                    if it >= 2:
                        c.wait("vector", self.s_sq[par], stq[it - 2][1])
                    vt = c.op("vector", "tensor_copy", hsl, wsl, inc=self.s_v)
                    c.wait("sync", self.s_v, vt)
                    st = c.dma("sync",
                               AP(cache, r0 * C + sl * W, [[C, 128], [1, W]]),
                               hsl, self.s_sq[par])
                    stq[it] = (vt, st)
                    it += 1
        self.wc_done = [(self.s_sq[p], self.count[self.s_sq[p]]) for p in range(2)]
        for p in range(2):
            c.wait("sync", self.s_sq[p])
        c.barrier(tag="wcache")

    def emit_quant(self):
        c = self
        NW = 2048
        for raw, cache, mcol, mhcol in [(self.t1_raw, self.t1c, "m1", "m1h"),
                                        (self.t2_raw, self.t2c, "m2", "m2h")]:
            nout = raw.shape[1]
            nic = raw.shape[0] // 128
            nsl = max(1, nout // NW)
            W = min(nout, NW)
            partial = c.bcol("scr", 0)
            c.wait("sync", self.s_v)
            red_t = {}
            it = 0
            for ic in range(nic):
                for sl in range(nsl):
                    par = it % 2
                    wsl = c.fv(par * 2048, 128, [[1, W]])
                    if it >= 2:
                        c.wait("sync", self.s_v, red_t[it - 2])
                    lt = c.dma("sync", wsl,
                               AP(raw, ic * 128 * nout + sl * W, [[nout, 128], [1, W]]),
                               self.s_wa[par])
                    c.wait("vector", self.s_wa[par], lt)
                    red = c.bcol("scr", 1)
                    c.op("vector", "tensor_reduce", red, wsl, AX, ALU.add,
                         apply_absolute_value=True)
                    if it == 0:
                        red_t[it] = c.op("vector", "tensor_copy", partial, red,
                                         inc=self.s_v)
                    else:
                        red_t[it] = c.op("vector", "tensor_tensor", partial,
                                         partial, red, ALU.add, inc=self.s_v)
                    it += 1
            c.wait("tensor", self.s_v)
            tot = self.pb[6]
            c.mm(tot[0:1, 0:1], partial, self.onescol[:], True, True, inc=self.s_mm)
            c.wait("vector", self.s_mm)
            m1p = AP(self.bias_sb, self.bofs["scr"] + 2, [[self.BW, 1], [1, 1]])
            c.op("vector", "tensor_single_scalar", m1p, tot[0:1, 0:1],
                 1.0 / float(nic * 128 * nout), ALU.mult)
            c.op("vector", "tensor_scalar_max", m1p, m1p, 1e-5, inc=self.s_v)
            c.wait("tensor", self.s_v)
            mbcf = self.pb[5][:, 0:1]
            c.mm(mbcf, self.onescf[:, 0:128],
                 AP(self.bias_sb, self.bofs["scr"] + 2, [[self.BW, 1], [1, 1]]),
                 True, True, inc=self.s_mm)
            c.wait("vector", self.s_mm)
            c.op("vector", "tensor_copy", c.bcol(mcol), mbcf)
            c.op("vector", "tensor_scalar_mul", c.bcol(mhcol), mbcf, 0.5, inc=self.s_v)
            c.wait("sync", self.s_v)
            for p in range(2):
                c.wait("vector", self.s_sq[p])
            stq = {}
            qt = {}
            pend = None
            it = 0

            def flush(pit):
                c.wait("sync", self.s_v, qt[pit])
                stq[pit] = c.dma("sync",
                                 AP(cache, pend[1], [[nout, 128], [1, W]]),
                                 c.hv((pit % 2) * W, 128, [[1, W]]),
                                 self.s_sq[pit % 2])

            for ic in range(nic):
                for sl in range(nsl):
                    par = it % 2
                    wsl = c.fv(par * 2048, 128, [[1, W]])
                    if it >= 2:
                        c.wait("sync", self.s_v, qt[it - 2])
                    lt = c.dma("sync", wsl,
                               AP(raw, ic * 128 * nout + sl * W, [[nout, 128], [1, W]]),
                               self.s_wa[par])
                    if pend is not None:
                        flush(pend[0])
                    tsl = c.hv(par * W, 128, [[1, W]])
                    gtp = c.hv(2 * W + par * W, 128, [[1, W]])
                    c.wait("vector", self.s_wa[par], lt)
                    if it >= 2:
                        c.wait("vector", self.s_sq[par], stq[it - 2])
                    c.op("vector", "tensor_scalar", tsl, wsl, c.bcol(mhcol), None,
                         ALU.is_gt)
                    c.op("vector", "tensor_scalar", gtp, wsl, -1.0, c.bcol(mhcol),
                         ALU.mult, ALU.is_gt)
                    qt[it] = c.op("vector", "tensor_tensor", tsl, tsl, gtp,
                                  ALU.subtract, inc=self.s_v)
                    pend = (it, ic * 128 * nout + sl * W)
                    it += 1
            flush(pend[0])
            for p in range(2):
                c.wait("sync", self.s_sq[p])
        c.barrier(tag="quant")

    def ln_stats(self, get_chunk, nch, ones_w, banks, tw, strips, gates=None):
        c = self
        MT = min(tw, 512)
        nmt = tw // MT
        for i in range(nch):
            par = i % 2
            is_bf, src = get_chunk(i)
            if gates is not None:
                c.wait("scalar", *gates[i])
                c.wait("tensor", *gates[i])
            if is_bf:
                xb = src
            else:
                if i >= 2:
                    c.wait("vector", self.s_mm, self.count[self.s_mm] - 1)
                xb = c.hv(strips + par * tw, 128, [[1, tw]])
                c.op("vector", "tensor_copy", xb, src, inc=self.s_v)
            xq_off = strips + 2 * tw + par * tw
            xq = c.hv(xq_off, 128, [[1, tw]])
            if i >= 2:
                c.wait("scalar", self.s_mm, self.count[self.s_mm] - 1)
            if not is_bf:
                c.wait("scalar", self.s_v)
            c.op("scalar", "activation", xq, xb, AF.Square, bias=0.0, scale=1.0,
                 inc=self.s_ev)
            if not is_bf:
                c.wait("tensor", self.s_v)
            c.wait("tensor", self.s_ev)
            for mt in range(nmt):
                mu_b, es_b = banks[mt]
                kw = dict(start=(i == 0), stop=(i == nch - 1))
                c.mm(self.pb[mu_b][:, 0:MT],  ones_w,
                     AP(xb.tensor, xb.offset + mt * MT, [[xb.ap[0][0], 128], [1, MT]]),
                     **kw)
                c.mm(self.pb[es_b][:, 0:MT], ones_w,
                     c.hv(xq_off + mt * MT, 128, [[1, MT]]),
                     inc=(self.s_mm if mt == nmt - 1 else None), **kw)

    def rstd(self, banks, tw, rs_off, mu_off):
        c = self
        MT = min(tw, 512)
        c.wait("vector", self.s_mm)
        c.wait("vector", self.s_g)
        for mt in range(tw // MT):
            mu_b, es_b = banks[mt]
            ve = c.fv(0, 128, [[1, MT]])
            rs = c.fv(rs_off + mt * MT, 128, [[1, MT]])
            mu_s = c.fv(mu_off + mt * MT, 128, [[1, MT]])
            if mt > 0:
                c.wait("vector", self.s_ev)
            c.op("vector", "tensor_copy", mu_s, self.pb[mu_b][:, 0:MT])
            c.op("vector", "tensor_single_scalar", ve, self.pb[es_b][:, 0:MT],
                 LN_EPS, ALU.add)
            c.op("vector", "tensor_tensor", rs, mu_s, mu_s, ALU.mult)
            c.op("vector", "tensor_tensor", ve, ve, rs, ALU.subtract, inc=self.s_v)
            c.wait("scalar", self.s_v)
            c.op("scalar", "activation", ve, ve, AF.Ln, bias=0.0, scale=1.0)
            c.op("scalar", "activation", rs, ve, AF.Exp, bias=0.0, scale=-0.5,
                 inc=self.s_ev)
        self._rstd_vg = self.count[self.s_v]
        c.wait("vector", self.s_ev)

    def ln_apply(self, dst_off, src_ap_fn, nch, tw, rs_off, mu_off,
                 gain=None, bias=None, stride=None, split=True):
        c = self
        MT = min(tw, 512)
        stride = stride if stride is not None else tw
        gates = []
        c.wait("gpsimd", self.s_ev)
        c.wait("gpsimd", self.s_v)
        c.wait("gpsimd", self.s_mm)
        c.wait("vector", self.s_ev)
        for i in range(nch):
            eng = ("vector", "gpsimd")[i % 2] if split else "vector"
            sem = self.s_v if eng == "vector" else self.s_g
            toff = 0 if eng == "vector" else self.AFW - MT
            for mt in range(tw // MT):
                o = mt * MT
                tmp = c.fv(toff, 128, [[1, MT]])
                dst = c.hv(dst_off + i * stride + o, 128, [[1, MT]])
                c.op(eng, "tensor_tensor", tmp, src_ap_fn(i, o, MT),
                     c.fv(mu_off + o, 128, [[1, MT]]), ALU.subtract)
                if gain is None:
                    c.op(eng, "tensor_tensor", dst, tmp,
                         c.fv(rs_off + o, 128, [[1, MT]]), ALU.mult, inc=sem)
                else:
                    c.op(eng, "tensor_tensor", tmp, tmp,
                         c.fv(rs_off + o, 128, [[1, MT]]), ALU.mult)
                    c.op(eng, "tensor_scalar", dst, tmp, c.bcol(gain, i),
                         c.bcol(bias, i), ALU.mult, ALU.add, inc=sem)
            gates.append((sem, self.count[sem]))
        return gates

    def emit_ffn(self, layer):
        c, T = self, self.T
        TP = min(T, 1024)
        NTP = T // TP
        MT = min(TP, 512)
        NMT = TP // MT
        HU = 0
        HS = HU + DC * TP
        HWA = HS + 4 * TP
        HWB = HWA + 2 * DC * 256
        HG = HWB + 2 * HC * 128
        assert HG + HC * TP <= self.AH
        F_RS = MT
        F_MU = F_RS + TP
        F_TB = F_MU + TP
        assert F_TB + 2 * MT <= self.AFW - MT

        w1 = self.t1c if layer == 0 else self.l1_w1c
        w2 = self.t2c if layer == 0 else self.l1_w2c
        ldsem = self.s_wa
        ldeng = "sync"
        for g in getattr(self, "wc_done", ()):
            c.wait(ldeng, *g)
        fb1 = "l0_fb1" if layer == 0 else "l1_fb1"
        fb2h = "l0_fb2h" if layer == 0 else "l1_fb2h"
        banks = [(0, 1), (2, 3)][:NMT]
        rb = (self.count[self.s_ev], self.count[self.s_v])

        for tp in range(NTP):
            t0 = tp * TP
            self.ln_stats(lambda i: (False, c.xv(i, t0, TP)), DC, self.ones1k[:],
                          banks, TP, HS)
            self.rstd(banks, TP, F_RS, F_MU)
            gu = self.ln_apply(HU, lambda i, o, w: c.xv(i, t0 + o, w), DC, TP,
                               F_RS, F_MU)
            if layer == 0:
                self.ln_stats(lambda i: (True, c.hv(HU + i * TP, 128, [[1, TP]])),
                              DC, self.ones1k[:], banks, TP, HS, gates=gu)
                self.rstd(banks, TP, F_RS, F_MU)
                gu = self.ln_apply(HU,
                                   lambda i, o, w: c.hv(HU + i * TP + o, 128, [[1, w]]),
                                   DC, TP, F_RS, F_MU, gain="l0_g1", bias="l0_b1v")

            nslA = HID // 256

            def mkA(s):
                par = s % 2
                dst = c.hv(HWA + par * DC * 256, 128, [[256, DC], [1, 256]])
                src = AP(w1, s * 256, [[HID, 128], [128 * HID, DC], [1, 256]])
                return dst, src

            ssA = SlabStream(c, ldeng, ldsem, nslA, mkA)
            rot = Rot(c, (4, 5), self.s_ev, bases=rb)
            ev_g = {}
            sq_tgt, sq_guard = {}, {}

            def ems_sq(j):
                if j >= 2:
                    c.wait("scalar", self.s_mm, sq_guard[j - 2])
                c.op("scalar", "activation",
                     c.hv(HS + 2 * TP + (j % 2) * TP, 128, [[1, TP]]),
                     c.hv(HG + j * TP, 128, [[1, TP]]), AF.Square,
                     bias=0.0, scale=1.0, inc=self.s_ev)
                sq_tgt[j] = (self.s_ev, self.count[self.s_ev])

            def ems_mm(j):
                c.wait("tensor", *sq_tgt[j])
                if j == 0:
                    c.wait("tensor", self.s_v, self._rstd_vg)
                for mt in range(NMT):
                    mu_b, es_b = banks[mt]
                    kw = dict(start=(j == 0), stop=(j == HC - 1))
                    c.mm(self.pb[mu_b][:, 0:MT], self.ones4k[:],
                         c.hv(HG + j * TP + mt * MT, 128, [[1, MT]]), **kw)
                    c.mm(self.pb[es_b][:, 0:MT], self.ones4k[:],
                         c.hv(HS + 2 * TP + (j % 2) * TP + mt * MT,
                              128, [[1, MT]]),
                         inc=(self.s_mm if mt == NMT - 1 else None), **kw)
                sq_guard[j] = self.count[self.s_mm]

            for s in range(nslA):
                par = s % 2
                ssA.wait_loaded(s)
                for oc2 in range(2):
                    oc = 2 * s + oc2
                    for mt in range(NMT):
                        b = rot.acquire()
                        for ic in range(DC):
                            c.wait("tensor", *gu[ic])
                            c.mm(self.pb[b][:, 0:MT],
                                 c.hv(HWA + par * DC * 256 + ic * 256 + oc2 * 128,
                                      128, [[1, 128]]),
                                 c.hv(HU + ic * TP + mt * MT, 128, [[1, MT]]),
                                 start=(ic == 0), stop=(ic == DC - 1),
                                 inc=(self.s_mm if ic == DC - 1 else None))
                        c.wait("scalar", self.s_mm)
                        scale = c.bcol("m1") if layer == 0 else 1.0
                        c.op("scalar", "activation",
                             c.hv(HG + oc * TP + mt * MT, 128, [[1, MT]]),
                             self.pb[b][:, 0:MT], AF.Gelu, bias=c.bcol(fb1, oc),
                             scale=scale, inc=self.s_ev)
                        rot.release(b)
                        ev_g[oc] = (self.s_ev, self.count[self.s_ev])
                    if layer == 0:
                        ems_sq(oc)
                        if oc >= 1:
                            ems_mm(oc - 1)
                ssA.done(s, self.s_mm)
            if layer == 0:
                ems_mm(HC - 1)
            rb = (self.count[self.s_ev], self.count[self.s_v])

            if layer == 0:
                self.rstd(banks, TP, F_RS, F_MU)
                gg = self.ln_apply(HG,
                                   lambda i, o, w: c.hv(HG + i * TP + o, 128, [[1, w]]),
                                   HC, TP, F_RS, F_MU, gain="l0_g2", bias="l0_b2v")
            else:
                gg = [ev_g[i] for i in range(HC)]

            nslB = DC

            def mkB(s):
                par = s % 2
                dst = c.hv(HWB + par * HC * 128, 128, [[128, HC], [1, 128]])
                src = AP(w2, s * 128, [[D, 128], [128 * D, HC], [1, 128]])
                return dst, src

            ssB = SlabStream(c, ldeng, ldsem, nslB, mkB)
            rot = Rot(c, (4, 5), self.s_v, bases=rb)
            for s in range(nslB):
                par = s % 2
                ssB.wait_loaded(s)
                ec = s
                for mt in range(NMT):
                    b = rot.acquire()
                    for hc in range(HC):
                        c.wait("tensor", *gg[hc])
                        c.mm(self.pb[b][:, 0:MT],
                             c.hv(HWB + par * HC * 128 + hc * 128, 128, [[1, 128]]),
                             c.hv(HG + hc * TP + mt * MT, 128, [[1, MT]]),
                             start=(hc == 0), stop=(hc == HC - 1),
                             inc=(self.s_mm if hc == HC - 1 else None))
                    c.wait("scalar", self.s_mm)
                    tb = c.fv(F_TB + (rot.i % 2) * MT, 128, [[1, MT]])
                    scale = c.bcol("m2h") if layer == 0 else 0.5
                    c.op("scalar", "mul", tb, self.pb[b][:, 0:MT], scale)
                    c.op("scalar", "add", tb, tb, c.bcol(fb2h, ec), inc=self.s_ev)
                    c.wait("vector", self.s_ev)
                    c.op("vector", "tensor_tensor", c.xv(ec, t0 + mt * MT, MT),
                         c.xv(ec, t0 + mt * MT, MT), tb, ALU.add, inc=self.s_v)
                    rot.release(b)
                ssB.done(s, self.s_mm)
            rb = (self.count[self.s_ev], self.count[self.s_v])

    def emit_attn0(self):
        c, T = self, self.T
        MT = min(T, 512)
        NMT = T // MT
        TC = T // 128
        QW = min(T, 256)
        NQ = T // QW
        KW = min(T, 512)
        NKT = T // KW
        HU = 0
        HK = HU + DC * T
        HV = HK + DC * T
        HWS = HV + TC * 1024
        HET = HWS + 2 * DC * 128
        HE = HET + TC * QW
        assert HE + T <= self.AH, (HE + T, self.AH)
        HPW = min(T, 1024)
        F_RS = MT
        F_MU = F_RS + HPW
        assert F_MU + HPW <= self.AFW

        for hp in range(T // HPW):
            t0 = hp * HPW
            nmt = HPW // MT
            bks = [(0, 1), (2, 3)][:nmt]
            self.ln_stats(lambda i, t0=t0: (False, c.xv(i, t0, HPW)), DC,
                          self.ones1k[:], bks, HPW, HK)
            self.rstd(bks, HPW, F_RS, F_MU)
            self.ln_apply(HU + t0, lambda i, o, w, t0=t0: c.xv(i, t0 + o, w),
                          DC, HPW, F_RS, F_MU, stride=T)
        c.wait("tensor", self.s_v)
        c.wait("tensor", self.s_g)

        def mk_ws(src_t, width):
            def mk(s):
                par = s % 2
                dst = c.hv(HWS + par * DC * 128, 128, [[128, DC], [1, width]])
                src = AP(src_t, s * width, [[D, 128], [128 * D, DC], [1, width]])
                return dst, src
            return mk

        for g in getattr(self, "wc_done", ()):
            c.wait("sync", *g)
        ss = SlabStream(c, "sync", self.s_wa, DC, mk_ws(self.l0_wkc, 128))
        rot = Rot(c, (4, 5), self.s_ev)
        for s in range(DC):
            par = s % 2
            ss.wait_loaded(s)
            oc = s
            for mt in range(NMT):
                b = rot.acquire()
                for ic in range(DC):
                    c.mm(self.pb[b][:, 0:MT],
                         c.hv(HWS + par * DC * 128 + ic * 128, 128, [[1, 128]]),
                         c.hv(HU + ic * T + mt * MT, 128, [[1, MT]]),
                         start=(ic == 0), stop=(ic == DC - 1),
                         inc=(self.s_mm if ic == DC - 1 else None))
                c.wait("scalar", self.s_mm)
                c.op("scalar", "add", c.hv(HK + oc * T + mt * MT, 128, [[1, MT]]),
                     self.pb[b][:, 0:MT], c.bcol("l0_bk", oc), inc=self.s_ev)
                rot.release(b)
            ss.done(s, self.s_mm)
        HVS = HWS
        assert HVS + 2 * DC * 512 <= self.AH

        def mk_vs(s):
            par = s % 2
            dst = c.hv(HVS + par * DC * 512, 128, [[512, DC], [1, 512]])
            src = AP(self.l0_wvc, s * 512, [[D, 128], [128 * D, DC], [1, 512]])
            return dst, src

        ss = SlabStream(c, "sync", self.s_wa, 2, mk_vs)
        rot = Rot(c, (4, 5), self.s_v)
        for s in range(2):
            par = s % 2
            ss.wait_loaded(s)
            for tc in range(TC):
                b = rot.acquire()
                for ic in range(DC):
                    c.mm(self.pb[b][:, 0:512],
                         c.hv(HU + ic * T + tc * 128, 128, [[1, 128]]),
                         c.hv(HVS + par * DC * 512 + ic * 512, 128, [[1, 512]]),
                         start=(ic == 0), stop=(ic == DC - 1),
                         inc=(self.s_mm if ic == DC - 1 else None))
                c.wait("vector", self.s_mm)
                c.op("vector", "tensor_tensor",
                     c.hv(HV + tc * 1024 + s * 512, 128, [[1, 512]]),
                     self.pb[b][:, 0:512],
                     AP(self.bvbc_sb, s * 512, [[D, 128], [1, 512]]),
                     ALU.add, inc=self.s_v)
                rot.release(b)
            ss.done(s, self.s_mm)
        c.wait("tensor", self.s_v)
        c.wait("scalar", self.s_v)

        QT = min(T, 256)
        NQT = T // QT
        mkq = mk_ws(self.l0_wqc, 128)
        qev = {}
        for qt in range(NQT):
            ss = SlabStream(c, "sync", self.s_wa, DC, mkq)
            for oc in range(DC):
                s = oc
                par = s % 2
                ss.wait_loaded(s)
                if qt > 0:
                    c.wait("tensor", *qev[oc])
                for ic in range(DC):
                    c.mm(self.pb[oc // 2][:, (oc % 2) * QT:(oc % 2) * QT + QT],
                         c.hv(HWS + par * DC * 128 + ic * 128, 128, [[1, 128]]),
                         c.hv(HU + ic * T + qt * QT, 128, [[1, QT]]),
                         start=(ic == 0), stop=(ic == DC - 1),
                         inc=(self.s_mm if ic == DC - 1 else None))
                ss.done(s, self.s_mm)
            c.wait("scalar", self.s_mm)
            for oc in range(DC):
                c.op("scalar", "add",
                     c.hv(HU + oc * T + qt * QT, 128, [[1, QT]]),
                     self.pb[oc // 2][:, (oc % 2) * QT:(oc % 2) * QT + QT],
                     c.bcol("l0_bq", oc), inc=self.s_ev)
                qev[oc] = (self.s_ev, self.count[self.s_ev])
        c.wait("tensor", self.s_ev)

        ag = {}
        rotAV = Rot(c, (4, 5), self.s_v)

        def emit_T(gq):
            qc = gq % 2
            heb = HE if gq % 2 == 0 else HWS
            c.wait("tensor", self.s_v, ag[gq]["p"])
            for kc8 in range((TC + 7) // 8):
                n8 = min(8, TC - kc8 * 8)
                for k8 in range(n8):
                    kc = kc8 * 8 + k8
                    c.op("tensor", "transpose",
                         self.ptr[:, k8 * 128:(k8 + 1) * 128],
                         c.hv(heb + kc * 128, 128, [[1, 128]]),
                         self.identh[:],
                         inc=(self.s_mm if k8 == n8 - 1 else None), drain=False)
                c.wait("vector", self.s_mm)
                for k8 in range(n8):
                    kc = kc8 * 8 + k8
                    c.op("vector", "tensor_copy",
                         c.hv(HET + kc * QW + qc * 128, 128, [[1, 128]]),
                         self.ptr[:, k8 * 128:(k8 + 1) * 128],
                         inc=(self.s_v if k8 == n8 - 1 else None))
                c.wait("tensor", self.s_v)
            ag[gq]["T"] = self.count[self.s_mm]

        def emit_AV(qs):
            for ec in range(DC):
                b = rotAV.acquire()
                for kc in range(TC):
                    c.mm(self.pb[b][:, 0:QW],
                         c.hv(HV + kc * 1024 + ec * 128, 128, [[1, 128]]),
                         c.hv(HET + kc * QW, 128, [[1, QW]]),
                         start=(kc == 0), stop=(kc == TC - 1),
                         inc=(self.s_mm if kc == TC - 1 else None))
                c.wait("vector", self.s_mm)
                c.op("vector", "tensor_tensor", c.xv(ec, qs * QW, QW),
                     c.xv(ec, qs * QW, QW), self.pb[b][:, 0:QW], ALU.add,
                     inc=self.s_v)
                rotAV.release(b)

        NGQ = T // 128
        for gq in range(NGQ):
            ag[gq] = {}
            q0 = gq * 128
            if gq > 0:
                c.wait("tensor", self.s_v, ag[gq - 1]["mx"])
                c.wait("tensor", self.s_ev, ag[gq - 1]["exp"])
            for kt in range(NKT):
                for ic in range(DC):
                    c.mm(self.pb[kt][:, 0:KW],
                         c.hv(HU + ic * T + q0, 128, [[1, 128]]),
                         c.hv(HK + ic * T + kt * KW, 128, [[1, KW]]),
                         start=(ic == 0), stop=(ic == DC - 1),
                         inc=(self.s_mm if ic == DC - 1 and kt == NKT - 1
                              else None))
            ag[gq]["s"] = self.count[self.s_mm]
            if gq > 0:
                emit_T(gq - 1)
            if gq % 2 == 0 and gq >= 2:
                emit_AV((gq - 2) // 2)
            c.wait("vector", self.s_mm, ag[gq]["s"])
            for kt in range(NKT):
                c.op("vector", "tensor_reduce", c.bcol("scr", kt),
                     self.pb[kt][:, 0:KW], AX, ALU.max)
            c.op("vector", "tensor_reduce", c.bcol("scr", 4),
                 c.bcol("scr", 0, NKT), AX, ALU.max)
            ag[gq]["mx"] = c.op("vector", "tensor_single_scalar", c.bcol("scr", 4),
                                c.bcol("scr", 4), -1.0 / 32.0, ALU.mult,
                                inc=self.s_v)
            c.wait("scalar", self.s_v, ag[gq]["mx"])
            heb = HE if gq % 2 == 0 else HWS
            if gq > 1:
                c.wait("scalar", self.s_mm, ag[gq - 2]["T"])
            for kt in range(NKT):
                c.op("scalar", "activation", c.hv(heb + kt * KW, 128, [[1, KW]]),
                     self.pb[kt][:, 0:KW], AF.Exp, bias=c.bcol("scr", 4),
                     scale=1.0 / 32.0, accum_out=c.bcol("scr", kt),
                     inc=self.s_ev)
            ag[gq]["exp"] = self.count[self.s_ev]
            c.wait("vector", self.s_ev, ag[gq]["exp"])
            ssum = c.bcol("scr", 5)
            c.op("vector", "tensor_reduce", ssum, c.bcol("scr", 0, NKT), AX,
                 ALU.add)
            c.op("vector", "reciprocal", ssum, ssum)
            ag[gq]["p"] = c.op("vector", "tensor_scalar_mul",
                               c.hv(heb, 128, [[1, T]]),
                               c.hv(heb, 128, [[1, T]]), ssum, inc=self.s_v)
        emit_T(NGQ - 1)
        emit_AV((NGQ - 2) // 2)
        c.wait("tensor", self.s_v)

    def emit_attn1(self):
        c, T = self, self.T
        MT = min(T, 512)
        NMT = T // MT
        TC = T // 128
        RN = TC * 64
        HU = 0
        HWQ = HU + DC * T
        HWU = HWQ + DC * 1024
        HWD = HWU + LC * 2048
        HKL = HWD + DC * 256
        HQC = HKL + LC * T
        HKV = HQC + 1024
        HQR = HKV + 2048
        HKR = HQR + 1024
        HPR = HKR + 1024
        HP = HPR + 4096
        HOC = HP + 256
        HOF = HOC + 2 * 1024
        HWO = HOF + DC * 512
        HPR2 = HWO + 2 * DC * 128
        assert HPR2 + 4096 <= self.AH, (HPR2 + 4096, self.AH)
        HPW = min(T, 1024)
        F_RS = MT
        F_MU = F_RS + HPW
        F_S = F_MU + HPW
        F_S2 = F_S + 256
        assert F_S2 + 256 <= self.AFW

        for hp in range(T // HPW):
            t0 = hp * HPW
            nmt = HPW // MT
            bks = [(0, 1), (2, 3)][:nmt]
            self.ln_stats(lambda i, t0=t0: (False, c.xv(i, t0, HPW)), DC,
                          self.ones1k[:], bks, HPW, HWQ)
            self.rstd(bks, HPW, F_RS, F_MU)
            self.ln_apply(HU + t0, lambda i, o, w, t0=t0: c.xv(i, t0 + o, w),
                          DC, HPW, F_RS, F_MU, stride=T)
        c.wait("tensor", self.s_v)
        c.wait("tensor", self.s_g)
        c.wait("gpsimd", self.s_v)

        for g in getattr(self, "wc_done", ()):
            c.wait("sync", *g)
        c.wait("sync", self.s_mm)
        c.dma("sync", c.hv(HWQ, 128, [[1024, DC], [1, 1024]]),
              AP(self.l1_wqc, 0, [[D, 128], [128 * D, DC], [1, D]]), self.s_wa[0])
        c.dma("sync", c.hv(HWU, 128, [[2048, LC], [1, 2048]]),
              AP(self.l1_wuc, 0, [[2 * D, 128], [128 * 2 * D, LC], [1, 2 * D]]),
              self.s_wa[0])
        c.dma("sync", c.hv(HWD, 128, [[256, DC], [1, 256]]),
              AP(self.l1_wdc, 0, [[DL, 128], [128 * DL, DC], [1, DL]]),
              self.s_wa[0])
        tg_w = self.count[self.s_wa[0]]
        c.wait("tensor", self.s_wa[0], tg_w)

        rot = Rot(c, (4, 5), self.s_ev)
        for lc in range(LC):
            for mt in range(NMT):
                b = rot.acquire()
                for ic in range(DC):
                    c.mm(self.pb[b][:, 0:MT],
                         c.hv(HWD + ic * 256 + lc * 128, 128, [[1, 128]]),
                         c.hv(HU + ic * T + mt * MT, 128, [[1, MT]]),
                         start=(ic == 0), stop=(ic == DC - 1),
                         inc=(self.s_mm if ic == DC - 1 else None))
                c.wait("scalar", self.s_mm)
                c.op("scalar", "add",
                     c.hv(HKL + lc * T + mt * MT, 128, [[1, MT]]),
                     self.pb[b][:, 0:MT], c.bcol("l1_bdown", lc), inc=self.s_ev)
                rot.release(b)
        c.wait("tensor", self.s_ev)

        gates = {}
        self._a1rb = (self.count[self.s_ev], self.count[self.s_v])

        def emit_proj(tc):
            rot = Rot(c, (4, 5), self.s_v, bases=self._a1rb)
            for ot in range(2):
                b = rot.acquire()
                for ic in range(DC):
                    c.mm(self.pb[b][:, 0:512],
                         c.hv(HU + ic * T + tc * 128, 128, [[1, 128]]),
                         c.hv(HWQ + ic * 1024 + ot * 512, 128, [[1, 512]]),
                         start=(ic == 0), stop=(ic == DC - 1),
                         inc=(self.s_mm if ic == DC - 1 else None))
                c.wait("vector", self.s_mm)
                c.op("vector", "tensor_tensor", c.hv(HQC + ot * 512, 128, [[1, 512]]),
                     self.pb[b][:, 0:512],
                     AP(self.bqbc_sb, ot * 512, [[D, 128], [1, 512]]), ALU.add,
                     inc=self.s_v)
                rot.release(b)
            if tc > 0:
                c.wait("vector", *gates[tc - 1]["avg"])
            for ot in range(4):
                b = rot.acquire()
                for lc in range(LC):
                    c.mm(self.pb[b][:, 0:512],
                         c.hv(HKL + lc * T + tc * 128, 128, [[1, 128]]),
                         c.hv(HWU + lc * 2048 + ot * 512, 128, [[1, 512]]),
                         start=(lc == 0), stop=(lc == LC - 1),
                         inc=(self.s_mm if lc == LC - 1 else None))
                c.wait("vector", self.s_mm)
                c.op("vector", "tensor_tensor", c.hv(HKV + ot * 512, 128, [[1, 512]]),
                     self.pb[b][:, 0:512],
                     AP(self.bupbc_sb, ot * 512, [[2 * D, 128], [1, 512]]), ALU.add,
                     inc=self.s_v)
                rot.release(b)
            self._a1rb = (self.count[self.s_ev], self.count[self.s_v])

        def elem_qk(tc, hq, eng, scr):
            sem = self.s_v if eng == "vector" else self.s_g
            qrep = AP(self.arena_h, HQR + hq * 256,
                      [[self.AH, 128], [64, 4], [0, NH], [1, 64]])
            krep = AP(self.arena_h, HKR,
                      [[self.AH, 128], [0, 4], [64, NH], [1, 64]])
            c.op(eng, "tensor_tensor", c.hv(scr, 128, [[1, 4096]]),
                 qrep, krep, ALU.mult)
            for half in (32, 16, 8, 4, 2, 1):
                a = AP(self.arena_h, scr, [[self.AH, 128], [64, 64], [1, half]])
                bb = AP(self.arena_h, scr + half,
                        [[self.AH, 128], [64, 64], [1, half]])
                c.op(eng, "tensor_tensor", a, a, bb, ALU.add)
            return (sem, c.op(eng, "tensor_copy",
                              c.fv(F_S + hq * 64, 128, [[1, 64]]),
                              AP(self.arena_h, scr, [[self.AH, 128], [64, 64]]),
                              inc=sem))

        def elem_av(tc, hq, eng, scr, hoc):
            sem = self.s_v if eng == "vector" else self.s_g
            prep = AP(self.arena_h, HP + hq * 64,
                      [[self.AH, 128], [16, 4], [1, NH], [0, 64]])
            vrep = AP(self.arena_h, HKV + 64,
                      [[self.AH, 128], [0, 4], [128, NH], [1, 64]])
            c.op(eng, "tensor_tensor", c.hv(scr, 128, [[1, 4096]]),
                 prep, vrep, ALU.mult)
            for halfg in (8, 4, 2, 1):
                a = AP(self.arena_h, scr,
                       [[self.AH, 128], [1024, 4], [64, halfg], [1, 64]])
                bb = AP(self.arena_h, scr + halfg * 64,
                        [[self.AH, 128], [1024, 4], [64, halfg], [1, 64]])
                c.op(eng, "tensor_tensor", a, a, bb, ALU.add)
            return (sem, c.op(eng, "tensor_copy",
                              c.hv(hoc + hq * 256, 128, [[64, 4], [1, 64]]),
                              AP(self.arena_h, scr,
                                 [[self.AH, 128], [1024, 4], [1, 64]]),
                              inc=sem))

        def emit_elem(tc):
            g = gates[tc] = {}
            hoc = HOC + (tc % 2) * 1024
            if tc > 0:
                c.wait("vector", *gates[tc - 1]["qkg"])
            ro = tc * 64
            cs_rep = AP(self.rope_cs, ro, [[RN, 128], [0, NH], [1, 64]])
            sn_rep = AP(self.rope_sn, ro, [[RN, 128], [0, NH], [1, 64]])
            q3 = c.hv(HQC, 128, [[64, NH], [1, 64]])
            qsw = AP(self.arena_h, HQC + 32,
                     [[self.AH, 128], [64, NH], [-32, 2], [1, 32]])
            c.op("vector", "tensor_tensor", c.hv(HQR, 128, [[64, NH], [1, 64]]),
                 q3, cs_rep, ALU.mult)
            c.op("vector", "tensor_tensor", c.hv(HPR2, 128, [[64, NH], [1, 64]]),
                 qsw, sn_rep, ALU.mult)
            c.op("vector", "tensor_tensor", c.hv(HQR, 128, [[1, 1024]]),
                 c.hv(HQR, 128, [[1, 1024]]), c.hv(HPR2, 128, [[1, 1024]]), ALU.add)
            k3 = c.hv(HKV, 128, [[128, NH], [1, 64]])
            ksw = AP(self.arena_h, HKV + 32,
                     [[self.AH, 128], [128, NH], [-32, 2], [1, 32]])
            c.op("vector", "tensor_tensor", c.hv(HKR, 128, [[64, NH], [1, 64]]),
                 k3, cs_rep, ALU.mult)
            c.op("vector", "tensor_tensor", c.hv(HPR2, 128, [[64, NH], [1, 64]]),
                 ksw, sn_rep, ALU.mult)
            c.op("vector", "tensor_tensor", c.hv(HKR, 128, [[1, 1024]]),
                 c.hv(HKR, 128, [[1, 1024]]), c.hv(HPR2, 128, [[1, 1024]]), ALU.add,
                 inc=self.s_v)
            rope_t = self.count[self.s_v]
            c.wait("gpsimd", self.s_v, rope_t)
            if tc > 0:
                c.wait("vector", self.s_ev, gates[tc - 1]["exp"])
                c.wait("gpsimd", self.s_ev, gates[tc - 1]["exp"])
                c.wait("gpsimd", self.s_v, gates[tc - 1]["p"])
            elem_qk(tc, 0, "vector", HPR2)
            elem_qk(tc, 1, "vector", HPR2)
            elem_qk(tc, 2, "vector", HPR2)
            g["qkg"] = elem_qk(tc, 3, "vector", HPR2)
            qkv_t = self.count[self.s_v]
            c.wait("vector", *g["qkg"])
            smax = c.fv(F_S2, 128, [[1, NH]])
            c.op("vector", "tensor_reduce", smax,
                 c.fv(F_S, 128, [[16, NH], [1, NH]]), AX, ALU.max, negate=True)
            c.op("vector", "tensor_tensor", c.fv(F_S, 128, [[16, NH], [1, NH]]),
                 c.fv(F_S, 128, [[16, NH], [1, NH]]),
                 AP(self.arena_f, F_S2, [[self.AFW, 128], [1, NH], [0, NH]]),
                 ALU.add, inc=self.s_v)
            c.wait("scalar", self.s_v)
            g["exp"] = c.op("scalar", "activation", c.fv(F_S, 128, [[1, 256]]),
                            c.fv(F_S, 128, [[1, 256]]), AF.Exp, bias=0.0,
                            scale=1.0 / 8.0, inc=self.s_ev)
            c.wait("vector", self.s_ev, g["exp"])
            ssum = c.fv(F_S2, 128, [[1, NH]])
            c.op("vector", "tensor_reduce", ssum,
                 c.fv(F_S, 128, [[16, NH], [1, NH]]), AX, ALU.add)
            c.op("vector", "reciprocal", ssum, ssum)
            g["p"] = c.op("vector", "tensor_tensor",
                          c.hv(HP, 128, [[16, NH], [1, NH]]),
                          c.fv(F_S, 128, [[16, NH], [1, NH]]),
                          AP(self.arena_f, F_S2, [[self.AFW, 128], [1, NH], [0, NH]]),
                          ALU.mult, inc=self.s_v)
            c.wait("gpsimd", self.s_v, g["p"])
            if tc > 1:
                c.wait("vector", self.s_mm, gates[tc - 2]["otr"])
                c.wait("gpsimd", self.s_mm, gates[tc - 2]["otr"])
            elem_av(tc, 0, "vector", HPR2, hoc)
            elem_av(tc, 1, "vector", HPR2, hoc)
            elem_av(tc, 2, "vector", HPR2, hoc)
            g["avv"] = g["avg"] = elem_av(tc, 3, "vector", HPR2, hoc)

        def emit_otrans(tc):
            g = gates[tc]
            hoc = HOC + (tc % 2) * 1024
            slab = tc % 4
            c.wait("tensor", *g["avg"])
            c.wait("tensor", *g["avv"])
            if tc > 0:
                c.wait("tensor", self.s_v, gates[tc - 1]["ofc"])
            for i8 in range(8):
                c.op("tensor", "transpose", self.ptr[:, i8 * 128:(i8 + 1) * 128],
                     c.hv(hoc + i8 * 128, 128, [[1, 128]]), self.identh[:],
                     inc=(self.s_mm if i8 == 7 else None), drain=False)
            g["otr"] = self.count[self.s_mm]
            c.wait("vector", self.s_mm, g["otr"])
            if tc >= 4 and slab == 0:
                c.wait("vector", self.s_mm, gates[tc - 1]["opr"])
            for i8 in range(8):
                c.op("vector", "tensor_copy",
                     c.hv(HOF + i8 * 512 + slab * 128, 128, [[1, 128]]),
                     self.ptr[:, i8 * 128:(i8 + 1) * 128],
                     inc=(self.s_v if i8 == 7 else None))
            g["ofc"] = self.count[self.s_v]

            if slab == 3 or tc == TC - 1:
                nsl = slab + 1
                t0 = (tc - slab) * 128

                def mkO(s):
                    par = s % 2
                    dst = c.hv(HWO + par * DC * 128, 128, [[128, DC], [1, 128]])
                    src = AP(self.l1_woc, s * 128,
                             [[D, 128], [128 * D, DC], [1, 128]])
                    return dst, src

                ssO = SlabStream(c, "sync", self.s_wa, DC, mkO)
                rotO = Rot(c, (4, 5), self.s_v, bases=self._a1rb)
                c.wait("tensor", self.s_v, g["ofc"])
                for s in range(DC):
                    par = s % 2
                    ec = s
                    ssO.wait_loaded(s)
                    b = rotO.acquire()
                    for ic in range(DC):
                        c.mm(self.pb[b][:, 0:nsl * 128],
                             c.hv(HWO + par * DC * 128 + ic * 128, 128, [[1, 128]]),
                             c.hv(HOF + ic * 512, 128, [[1, nsl * 128]]),
                             start=(ic == 0), stop=(ic == DC - 1),
                             inc=(self.s_mm if ic == DC - 1 else None))
                    c.wait("vector", self.s_mm)
                    c.op("vector", "scalar_tensor_tensor", c.xv(ec, t0, nsl * 128),
                         self.pb[b][:, 0:nsl * 128], c.bcol("l1_bout", ec),
                         c.xv(ec, t0, nsl * 128), ALU.add, ALU.add, inc=self.s_v)
                    rotO.release(b)
                    ssO.done(s, self.s_mm)
                g["opr"] = self.count[self.s_mm]
                self._a1rb = (self.count[self.s_ev], self.count[self.s_v])
            else:
                g["opr"] = gates[tc - 1]["opr"] if tc > 0 else 0

        for tc in range(TC):
            emit_proj(tc)
            if tc > 0:
                emit_otrans(tc - 1)
            emit_elem(tc)
        emit_otrans(TC - 1)
        c.wait("tensor", self.s_v)


def build(T=2048, phases=("f0", "a0", "f0", "f1", "a1", "f1"), taps=()):
    b = Builder(T, phases, taps)
    b.declare()
    b.emit_setup()
    if "f0" in phases:
        b.emit_quant()
    for pi, ph in enumerate(phases):
        {"f0": lambda: b.emit_ffn(0),
         "a0": b.emit_attn0,
         "f1": lambda: b.emit_ffn(1),
         "a1": b.emit_attn1}[ph]()
        b.barrier(tag=f"ph{pi}")
        if pi in b.tap_t:
            T_ = b.T
            b.dma("sync", AP(b.tap_t[pi], 0, [[T_, 128], [128 * T_, DC], [1, T_]]),
                  AP(b.x_sb, 0, [[DC * T_, 128], [T_, DC], [1, T_]]), b.s_st)
            b.wait("sync", b.s_st)
            b.barrier(tag=f"tap{pi}")
    T_ = b.T
    b.dma("sync", AP(b.y_t, 0, [[T_, 128], [128 * T_, DC], [1, T_]]),
          AP(b.x_sb, 0, [[DC * T_, 128], [T_, DC], [1, T_]]), b.s_st)
    b.wait("sync", b.s_st)
    return b.finalize()


def _stage(inputs):
    asc = np.ascontiguousarray
    x = np.asarray(inputs["L0_attn_x"], np.float32)
    ncore = x.shape[0]

    def rep(v):
        v = np.asarray(v, np.float32)
        return asc(np.broadcast_to(v[None, :], (128, v.shape[0])))

    def col(v, w):
        return asc(np.asarray(v, np.float32).reshape(w, 128).T)

    shared = {
        "l0_wqt": asc(np.asarray(inputs["L0_attn_Wq"], np.float32).T),
        "l0_wkt": asc(np.asarray(inputs["L0_attn_Wk"], np.float32).T),
        "l0_wvt": asc(np.asarray(inputs["L0_attn_Wv"], np.float32).T),
        "l0_bq": col(inputs["L0_attn_bq"], DC),
        "l0_bk": col(inputs["L0_attn_bk"], DC),
        "l0_bv_bc": rep(inputs["L0_attn_bv"]),
        "t1_raw": asc(np.asarray(inputs["L0_ffn_w1"], np.float32).T),
        "t2_raw": asc(np.asarray(inputs["L0_ffn_w2"], np.float32).T),
        "l0_g1": col(inputs["L0_ffn_ln1_g"], DC),
        "l0_b1v": col(inputs["L0_ffn_ln1_b"], DC),
        "l0_g2": col(inputs["L0_ffn_ln2_g"], HC),
        "l0_b2v": col(inputs["L0_ffn_ln2_b"], HC),
        "l0_fb1": col(inputs["L0_ffn_b1"], HC),
        "l0_fb2": col(inputs["L0_ffn_b2"], DC),
        "l1_wq": asc(np.asarray(inputs["L1_attn_Wq"], np.float32)),
        "l1_bq_bc": rep(inputs["L1_attn_bq"]),
        "l1_wdown": asc(np.asarray(inputs["L1_attn_Wdown"], np.float32)),
        "l1_bdown": col(inputs["L1_attn_bdown"], LC),
        "l1_wup": asc(np.asarray(inputs["L1_attn_Wup"], np.float32)),
        "l1_bup_bc": rep(inputs["L1_attn_bup"]),
        "l1_wout": asc(np.asarray(inputs["L1_attn_Wout"], np.float32)),
        "l1_bout": col(inputs["L1_attn_bout"], DC),
        "l1_w1": asc(np.asarray(inputs["L1_ffn_w1"], np.float32)),
        "l1_fb1": col(inputs["L1_ffn_b1"], HC),
        "l1_w2": asc(np.asarray(inputs["L1_ffn_w2"], np.float32)),
        "l1_fb2": col(inputs["L1_ffn_b2"], DC),
    }
    T = x.shape[1]
    TC = T // 128
    try:
        import jax
        import jax.numpy as jnp
        with jax.default_device(jax.devices("cpu")[0]):
            inv = 1.0 / 10000.0 ** (jnp.arange(0, DH, 2, dtype=jnp.float32) / DH)
            t = jnp.arange(T, dtype=jnp.float32)
            fr = t[:, None] * inv[None, :]
            snf = np.asarray(jnp.sin(fr), dtype=np.float32)
            csf = np.asarray(jnp.cos(fr), dtype=np.float32)
    except Exception:
        e = np.arange(0, DH, 2, dtype=np.float32) / np.float32(DH)
        invf = (np.float32(1.0) / np.power(np.float32(10000.0), e)).astype(np.float32)
        fr = (np.arange(T, dtype=np.float32)[:, None] * invf[None, :]).astype(np.float32)
        snf = np.sin(fr).astype(np.float32)
        csf = np.cos(fr).astype(np.float32)
    sn_tab = np.zeros((128, TC * 64), np.float32)
    cs_tab = np.zeros((128, TC * 64), np.float32)
    for tc in range(TC):
        rows = snf[tc * 128:(tc + 1) * 128]
        sn_tab[:, tc * 64:tc * 64 + 32] = -rows
        sn_tab[:, tc * 64 + 32:tc * 64 + 64] = rows
        cs_tab[:, tc * 64:tc * 64 + 32] = csf[tc * 128:(tc + 1) * 128]
        cs_tab[:, tc * 64 + 32:tc * 64 + 64] = csf[tc * 128:(tc + 1) * 128]
    shared["rope_snh"] = sn_tab
    shared["rope_csh"] = cs_tab
    in_maps = []
    for i in range(ncore):
        m = dict(shared)
        m["x_t"] = asc(x[i].T)
        in_maps.append(m)
    return in_maps


def kernel(**inputs):
    T = inputs["L0_attn_x"].shape[1]
    in_maps = _stage(inputs)
    nc = build(T=T)
    res = run_bass_kernel_spmd(nc, in_maps, core_ids=list(range(len(in_maps))))
    out = np.stack([np.ascontiguousarray(np.asarray(r["y_t"]).T)
                    for r in res.results], axis=0)
    return out.astype(np.float32)
